# version 04 (final)
# baseline (speedup 1.0000x reference)
import numpy as np
import ml_dtypes
from collections import defaultdict
from contextlib import ExitStack

import concourse.bass as bass
import concourse.mybir as mybir
from concourse.bass_utils import run_bass_kernel_spmd

F32 = mybir.dt.float32
BF16 = mybir.dt.bfloat16
I32 = mybir.dt.int32
AF = mybir.ActivationFunctionType
ALU = mybir.AluOpType
X = mybir.AxisListType.X
NP_BF16 = ml_dtypes.bfloat16

D = 2048
TPC = 1024
NCORE = 8
NH = 16
DH = 128
S = 4096
BATCH = 2
MLP0 = 16384
L1L = 1024
HID = [1024, 2048, 3072, 4096, 1024, 2048, 3072, 4096]
COST_LAMBDA = 1e-07
NT = D // 128
EPS_LN = 1e-5
EPS_ATT0 = 1e-6
GELU_C = 0.7978845608028654

SIM_MODE = [False]
DEBUG_A = [False]


COMPUTE_BODY = {"pe": "tensor", "dve": "vector", "act": "scalar", "pool": "gpsimd"}


class Op:
    __slots__ = ("domain", "emit", "reads", "writes", "inc", "inc_amt", "waits", "tag")

    def __init__(self, domain, emit, reads, writes, inc, inc_amt, tag):
        self.domain = domain
        self.emit = emit
        self.reads = tuple(reads)
        self.writes = tuple(writes)
        self.inc = inc
        self.inc_amt = inc_amt
        self.waits = []
        self.tag = tag


class Prog:
    def __init__(self):
        self.ops = []
        self.body_of = dict(COMPUTE_BODY)
        self.dma_domains = {}
        self.final_counts = None

    def dma_domain(self, name, body):
        self.dma_domains[name] = body
        self.body_of[name] = body
        return name

    def add(self, domain, emit, reads=(), writes=(), inc=True, tag=""):
        assert domain in self.body_of, domain
        amt = 16 if domain in self.dma_domains else 1
        op = Op(domain, emit, reads, writes, inc, amt, tag)
        self.ops.append(op)
        return op

    def resolve(self):
        cnt = defaultdict(int)
        last_write = {}
        readers = defaultdict(list)
        pending = defaultdict(list)
        watermark = defaultdict(lambda: defaultdict(int))
        prev_dma = {}

        for op in self.ops:
            needs = {}

            def need(rec, why, key):
                d, s2, _src = rec
                if d == "pe" and op.domain == "pe":
                    return
                if s2 is None:
                    raise RuntimeError(
                        f"op {op.tag!r} ({op.domain}) {why} on {key!r} from "
                        f"unstamped {d} op {_src.tag!r}: mark it inc=True")
                if watermark[op.domain][d] < s2:
                    needs[d] = max(needs.get(d, 0), s2)

            if op.domain in self.dma_domains and op.domain in prev_dma:
                need(prev_dma[op.domain], "STREAM", op.domain)
            for k in op.reads:
                if isinstance(k, tuple) and len(k) == 2 and k[0] == "__dom__":
                    d = k[1]
                    if cnt[d] > watermark[op.domain][d]:
                        needs[d] = max(needs.get(d, 0), cnt[d])
                    continue
                if k in last_write:
                    need(last_write[k], "RAW", k)
            for k in op.writes:
                if k in last_write:
                    need(last_write[k], "WAW", k)
                for rec in readers[k]:
                    need(rec, "WAR", k)

            for d, s2 in sorted(needs.items()):
                op.waits.append((d, s2))
                watermark[op.domain][d] = max(watermark[op.domain][d], s2)

            rec = [op.domain, None, op]
            if op.inc:
                cnt[op.domain] += op.inc_amt
                rec[1] = cnt[op.domain]
                for pnd in pending[op.domain]:
                    pnd[1] = cnt[op.domain]
                pending[op.domain] = []
            else:
                pending[op.domain].append(rec)
            if op.domain in self.dma_domains:
                prev_dma[op.domain] = rec
            for k in op.writes:
                last_write[k] = rec
                readers[k] = []
            for k in op.reads:
                if not (isinstance(k, tuple) and k and k[0] == "__dom__"):
                    readers[k].append(rec)
        self.final_counts = dict(cnt)

    def build(self, nc, es):
        self.resolve()
        sems = {}
        for d in self.body_of:
            sems[d] = es.enter_context(nc.semaphore(f"sem_{d}"))
        block = es.enter_context(nc.Block())
        prog = self

        def mk_body(body_name):
            def body(eng):
                for op in prog.ops:
                    if prog.body_of[op.domain] != body_name:
                        continue
                    for d, v in op.waits:
                        eng.wait_ge(sems[d], v)
                    inst = op.emit(eng)
                    if op.inc:
                        inst.then_inc(sems[op.domain], op.inc_amt)
            return body
        block.sync(mk_body("sync"))
        block.tensor(mk_body("tensor"))
        block.vector(mk_body("vector"))
        block.scalar(mk_body("scalar"))
        block.gpsimd(mk_body("gpsimd"))
        return sems


class Ctx:

    def __init__(self, name):
        self.nc = bass.Bass("TRN2", target_bir_lowering=False, debug=False)
        self.p = Prog()
        self.es = ExitStack()
        self.ins = {}
        self.outs = {}
        self._uid = 0

    def uid(self):
        self._uid += 1
        return self._uid

    def inp(self, name, shape, dt=F32):
        ap = self.nc.dram_tensor(name, list(shape), dt, kind="ExternalInput").ap()
        self.ins[name] = ap
        return ap

    def out(self, name, shape, dt=F32):
        ap = self.nc.dram_tensor(name, list(shape), dt, kind="ExternalOutput").ap()
        self.outs[name] = ap
        return ap

    def sb(self, name, shape, dt=F32):
        return self.es.enter_context(self.nc.sbuf_tensor(name, list(shape), dt))

    def psum(self, name, shape, dt=F32):
        return self.es.enter_context(self.nc.psum_tensor(name, list(shape), dt))

    def finish(self):
        doms = [("__dom__", d) for d in self.p.dma_domains]
        self.p.add("pool", lambda e: e.nop(), reads=doms, inc=False, tag="final_fence")
        self.p.build(self.nc, self.es)
        self.es.close()
        return self.nc


def run_launch(ctx, in_maps, core_ids):
    if SIM_MODE[0]:
        import concourse.bass_interp as bass_interp
        sim = bass_interp.MultiCoreSim(ctx.nc, len(core_ids))
        for c, m in enumerate(in_maps):
            for k, v in m.items():
                sim.cores[c].tensor(k)[:] = v
        sim.simulate()
        return [{name: np.asarray(sim.cores[c].mem_tensor(name)).reshape(
                     [int(s) for s in ctx.outs[name].shape])
                 for name in ctx.outs} for c in range(len(core_ids))]
    res = run_bass_kernel_spmd(ctx.nc, in_maps, core_ids=core_ids)
    return res.results



def emit_loads(ctx, dom, pairs, tag="ld"):
    for sb_ap, dr_ap, key in pairs:
        ctx.p.add(dom, (lambda s=sb_ap, d=dr_ap: (lambda e: e.dma_start(out=s, in_=d)))(),
                  writes=[key], tag=f"{tag}_{key}")


def emit_dequant_tile(ctx, pk, pk_key, nat, nat_key, scl, scl_key, scr, scrtag=0):
    p = ctx.p
    u = ctx.uid()

    def A(dom, emit, reads, writes, tag):
        p.add(dom, emit, reads=reads, writes=writes, tag=f"dq{u}_{tag}")
    ilo, ihi, C, T = scr["ilo"], scr["ihi"], scr["C"], scr["T"]
    Rf, Mf, Sf, Af, Bf = scr["Rf"], scr["Mf"], scr["Sf"], scr["Af"], scr["Bf"]
    k = lambda t: ("scrS_" + t) if t in ("ilo", "ihi") else f"scr{scrtag}_{t}"
    A("dve", lambda e: e.tensor_scalar(ilo[:], pk, 15, None, ALU.bitwise_and), [pk_key], [k("ilo")], "lo")
    A("dve", lambda e: e.tensor_scalar(ihi[:], pk, 4, 15, ALU.logical_shift_right, ALU.bitwise_and), [pk_key], [k("ihi")], "hi")
    A("pool", lambda e: e.tensor_copy(C[:, 0:2048:2], ilo[:]), [k("ilo")], [k("C")], "il")
    A("pool", lambda e: e.tensor_copy(C[:, 1:2048:2], ihi[:]), [k("ihi")], [k("C")], "ih")
    A("dve", lambda e: e.tensor_scalar(T[:], C[:], 7, None, ALU.bitwise_and), [k("C")], [k("T")], "r7")
    A("pool", lambda e: e.tensor_copy(Rf[:], T[:]), [k("T")], [k("Rf")], "rf")
    A("dve", lambda e: e.tensor_scalar(T[:], C[:], 1, None, ALU.bitwise_and), [k("C")], [k("T")], "m1")
    A("pool", lambda e: e.tensor_copy(Mf[:], T[:]), [k("T")], [k("Mf")], "mf")
    A("dve", lambda e: e.tensor_scalar(T[:], C[:], 3, None, ALU.logical_shift_right), [k("C")], [k("T")], "s3")
    A("pool", lambda e: e.tensor_copy(Sf[:], T[:]), [k("T")], [k("Sf")], "sf")
    A("pool", lambda e: e.tensor_sub(Af[:], Rf[:], Mf[:]), [k("Rf"), k("Mf")], [k("Af")], "u")
    A("pool", lambda e: e.tensor_mul(Bf[:], Af[:], Af[:]), [k("Af")], [k("Bf")], "u2")
    A("dve", lambda e: e.scalar_tensor_tensor(Bf[:], Af[:], -2.0, Bf[:], ALU.mult, ALU.add), [k("Af"), k("Bf")], [k("Bf")], "a")
    A("pool", lambda e: e.tensor_scalar(Bf[:], Bf[:], 0.0625, 0.5, ALU.mult, ALU.add), [k("Bf")], [k("Bf")], "q")
    A("dve", lambda e: e.scalar_tensor_tensor(Bf[:], Mf[:], 2.0, Bf[:], ALU.add, ALU.mult), [k("Mf"), k("Bf")], [k("Bf")], "w")
    A("pool", lambda e: e.tensor_scalar(Rf[:], Af[:], 1.5, None, ALU.is_le), [k("Af")], [k("Rf")], "z")
    A("pool", lambda e: e.tensor_sub(Bf[:], Bf[:], Rf[:]), [k("Bf"), k("Rf")], [k("Bf")], "v")
    A("pool", lambda e: e.tensor_scalar(Sf[:], Sf[:], -2.0, 1.0, ALU.mult, ALU.add), [k("Sf")], [k("Sf")], "sg")
    A("pool", lambda e: e.tensor_mul(Bf[:], Bf[:], Sf[:]), [k("Bf"), k("Sf")], [k("Bf")], "vs")
    scl_b = scl.unsqueeze(2).broadcast_to([128, 64, 32])
    B3 = Bf[:].rearrange("p (b j) -> p b j", j=32)
    nat3 = nat.rearrange("p (b j) -> p b j", j=32)
    A("dve", lambda e: e.tensor_tensor(nat3, B3, scl_b, ALU.mult),
      [k("Bf"), scl_key], [nat_key], "fin")


def emit_transpose_out(ctx, nat_tiles, nat_keys, n_rowtiles, n_coltiles, out_dram,
                       ident_bf, ident_key, strips, pb, pb_keys, doms, tag):
    p = ctx.p
    for ct in range(n_coltiles):
        strip, strip_key = strips[ct % 2]
        for rt in range(n_rowtiles):
            bank = pb[rt % 2]
            bkey = pb_keys[rt % 2]
            off = (rt // 2) % 4
            p.add("pe", (lambda ct=ct, rt=rt, bank=bank, off=off:
                         (lambda e: e.transpose(bank[:, 128 * off:128 * (off + 1)],
                                                nat_tiles[rt][:, 128 * ct:128 * (ct + 1)],
                                                ident_bf)))(),
                  reads=[nat_keys[rt], ident_key], writes=[bkey], tag=f"{tag}_tr{ct}_{rt}")
            p.add("dve", (lambda rt=rt, bank=bank, off=off, strip=strip:
                          (lambda e: e.tensor_copy(strip[:, 128 * rt:128 * (rt + 1)],
                                                   bank[:, 128 * off:128 * off + 128])))(),
                  reads=[bkey], writes=[strip_key], tag=f"{tag}_ev{ct}_{rt}")
        p.add(doms[ct % 2], (lambda ct=ct, strip=strip: (lambda e: e.dma_start(
            out=out_dram[128 * ct:128 * (ct + 1), :], in_=strip[:, 0:n_rowtiles * 128])))(),
            reads=[strip_key], tag=f"{tag}_st{ct}")



def build_prep():
    ctx = Ctx("prep")
    p = ctx.p
    p.dma_domain("qa", "sync")
    p.dma_domain("qb", "scalar")
    p.dma_domain("qo", "gpsimd")
    p.dma_domain("qp", "gpsimd")
    p.dma_domain("qr", "sync")

    pk_g = ctx.inp("pk_g", [2048, 1024], I32)
    sc_g = ctx.inp("sc_g", [2048, 64])
    pk_u = ctx.inp("pk_u", [2048, 1024], I32)
    sc_u = ctx.inp("sc_u", [2048, 64])
    pk_d = ctx.inp("pk_d", [2048, 1024], I32)
    sc_d = ctx.inp("sc_d", [2048, 64])
    aw = ctx.inp("aw", [8, 256, 2048])
    ident_in = ctx.inp("ident_in", [128, 128])
    l1_names = ["down_w", "up_w", "router_w", "shared_w1", "shared_w2", "core_w"] + \
               [f"e{e}_w{j}" for e in range(8) for j in (1, 2)]
    l1_shapes = {"down_w": (D, L1L), "up_w": (L1L, D), "router_w": (D, 8),
                 "shared_w1": (L1L, L1L), "shared_w2": (L1L, L1L), "core_w": (D, D)}
    for e in range(8):
        l1_shapes[f"e{e}_w1"] = (L1L, HID[e])
        l1_shapes[f"e{e}_w2"] = (HID[e], L1L)
    l1_in = {}
    for nm in l1_names:
        r, c = l1_shapes[nm]
        l1_in[nm] = ctx.inp("l1_" + nm, [r // 8, c])

    wgt_s = ctx.out("wgt_s", [2048, 2048], BF16)
    wut_s = ctx.out("wut_s", [2048, 2048], BF16)
    wdt_s = ctx.out("wdt_s", [2048, 2048], BF16)
    awt_s = ctx.out("awt_s", [8, 2048, 256], BF16)
    l1_out = {}
    for nm in l1_names:
        r, c = l1_shapes[nm]
        l1_out[nm] = ctx.out("o_" + nm, [r // 8, c], BF16)

    nat = [ctx.sb(f"nat{i}", [128, 2048], BF16) for i in range(16)]
    nat_keys = [f"nat{i}" for i in range(16)]
    pkbuf = [ctx.sb(f"pkb{i}", [128, 1024], I32) for i in range(3)]
    sclbuf = [ctx.sb(f"sclb{i}", [128, 64]) for i in range(3)]
    scrs = []
    shared_int = {}
    for nm, dt in [("ilo", I32), ("ihi", I32)]:
        shape = [128, 1024]
        shared_int[nm] = ctx.sb("scrS_" + nm, shape, dt)
    for s2 in range(2):
        scr = dict(shared_int)
        for nm, dt in [("C", I32), ("T", I32),
                       ("Rf", F32), ("Mf", F32), ("Sf", F32), ("Af", F32), ("Bf", F32)]:
            scr[nm] = ctx.sb(f"scr{s2}_" + nm, [128, 2048], dt)
        scrs.append(scr)
    ident = ctx.sb("ident", [128, 128])
    ident_bf = ctx.sb("ident_bf", [128, 128], BF16)
    strip = ctx.sb("strip", [128, 2048], BF16)
    strip2 = ctx.sb("strip2", [128, 2048], BF16)
    awbf = [nat[0], nat[1]]
    l1buf = scrs[1]["Rf"]
    awbuf = scrs[0]["Rf"]
    l1bf = nat[2]
    pb = [ctx.psum(f"pb{i}", [128, 1024], BF16) for i in range(2)]
    pb_keys = ["pb0", "pb1"]

    emit_loads(ctx, "qa", [(ident[:], ident_in, "ident")])
    p.add("dve", lambda e: e.tensor_copy(ident_bf[:], ident[:]), reads=["ident"],
          writes=["ident_bf"], tag="identcast")

    for mat, (pk_in, sc_in, out_dram) in enumerate(
            [(pk_g, sc_g, wgt_s), (pk_u, sc_u, wut_s), (pk_d, sc_d, wdt_s)]):
        for i in range(16):
            bp = pkbuf[i % 3]
            bs = sclbuf[i % 3]
            kp, ks = f"pk{i % 3}", f"sc{i % 3}"
            emit_loads(ctx, "qa", [(bp[:], pk_in[128 * i:128 * (i + 1), :], kp)], tag=f"m{mat}pk{i}")
            emit_loads(ctx, "qb", [(bs[:], sc_in[128 * i:128 * (i + 1), :], ks)], tag=f"m{mat}sc{i}")
            emit_dequant_tile(ctx, bp[:], kp, nat[i][:], nat_keys[i], bs[:, :], ks,
                              scrs[i % 2], scrtag=i % 2)
        emit_transpose_out(ctx, nat, nat_keys, 16, 16, out_dram, ident_bf[:], "ident_bf",
                           [(strip, "strip"), (strip2, "strip2")], pb, pb_keys,
                           ["qo", "qp"], f"mat{mat}")

    for j in range(8):
        for half in range(2):
            p.add("qa", (lambda j=j, half=half: (lambda e: e.dma_start(
                out=awbuf[:], in_=aw[j, 128 * half:128 * (half + 1), :])))(),
                writes=["scr1_Rf" if False else "scr0_Rf"], tag=f"aw{j}_{half}")
            p.add("dve", (lambda half=half: (lambda e: e.tensor_copy(awbf[half][:], awbuf[:])))(),
                  reads=["scr0_Rf"], writes=[nat_keys[half]], tag=f"awc{j}_{half}")
        for ct in range(16):
            for rt in range(2):
                bank, bkey = pb[rt], pb_keys[rt]
                p.add("pe", (lambda ct=ct, rt=rt, bank=bank:
                             (lambda e: e.transpose(bank[:, 0:128],
                                                    awbf[rt][:, 128 * ct:128 * (ct + 1)],
                                                    ident_bf[:])))(),
                      reads=[nat_keys[rt], "ident_bf"], writes=[bkey], tag=f"awtr{j}_{ct}_{rt}")
                p.add("dve", (lambda ct=ct, rt=rt, bank=bank:
                              (lambda e: e.tensor_copy(strip[:, 128 * rt:128 * (rt + 1)],
                                                       bank[:, 0:128])))(),
                      reads=[bkey], writes=["strip"], tag=f"awev{j}_{ct}_{rt}")
            p.add(["qo", "qp"][ct % 2], (lambda j=j, ct=ct: (lambda e: e.dma_start(
                out=awt_s[j, 128 * ct:128 * (ct + 1), :], in_=strip[:, 0:256])))(),
                reads=["strip"], tag=f"awst{j}_{ct}")

    for nm in l1_names:
        r, c = l1_shapes[nm]
        rows = r // 8
        for i in range(0, rows, 128):
            nrow = min(128, rows - i)
            for j in range(0, c, 2048):
                ncol = min(2048, c - j)
                p.add("qr", (lambda nm=nm, i=i, j=j, nrow=nrow, ncol=ncol: (lambda e: e.dma_start(
                    out=l1buf[0:nrow, 0:ncol], in_=l1_in[nm][i:i + nrow, j:j + ncol])))(),
                    writes=["scr1_Rf"], tag=f"l1ld_{nm}_{i}_{j}")
                p.add("dve", (lambda nrow=nrow, ncol=ncol: (lambda e: e.tensor_copy(
                    l1bf[0:nrow, 0:ncol], l1buf[0:nrow, 0:ncol])))(),
                    reads=["scr1_Rf"], writes=[nat_keys[2]], tag=f"l1c_{nm}_{i}_{j}")
                p.add("qo", (lambda nm=nm, i=i, j=j, nrow=nrow, ncol=ncol: (lambda e: e.dma_start(
                    out=l1_out[nm][i:i + nrow, j:j + ncol], in_=l1bf[0:nrow, 0:ncol])))(),
                    reads=[nat_keys[2]], tag=f"l1st_{nm}_{i}_{j}")

    ctx.finish()
    return ctx


def host_prep_inputs(inputs):
    maps = []
    ident = np.eye(128, dtype=np.float32)
    awstack = np.stack([inputs["L0_attn_Wq"], inputs["L0_attn_Wk"], inputs["L0_attn_Wv"],
                        inputs["L0_attn_Wo"], inputs["L1_attn_Wq"], inputs["L1_attn_Wk"],
                        inputs["L1_attn_Wv"], inputs["L1_attn_Wo"]])
    l1_names = ["down_w", "up_w", "router_w", "shared_w1", "shared_w2", "core_w"] + \
               [f"e{e}_w{j}" for e in range(8) for j in (1, 2)]
    for c in range(NCORE):
        m = {
            "pk_g": np.ascontiguousarray(inputs["L0_ffn_gate_W_packed"][2048 * c:2048 * (c + 1)]),
            "sc_g": np.ascontiguousarray(inputs["L0_ffn_gate_scales"][2048 * c:2048 * (c + 1)]),
            "pk_u": np.ascontiguousarray(inputs["L0_ffn_up_W_packed"][2048 * c:2048 * (c + 1)]),
            "sc_u": np.ascontiguousarray(inputs["L0_ffn_up_scales"][2048 * c:2048 * (c + 1)]),
            "pk_d": np.ascontiguousarray(inputs["L0_ffn_down_W_packed"][:, 1024 * c:1024 * (c + 1)]),
            "sc_d": np.ascontiguousarray(inputs["L0_ffn_down_scales"][:, 64 * c:64 * (c + 1)]),
            "aw": np.ascontiguousarray(awstack[:, 256 * c:256 * (c + 1), :]),
            "ident_in": ident,
        }
        for nm in l1_names:
            w = inputs["L1_ffn_" + nm]
            r = w.shape[0] // 8
            m["l1_" + nm] = np.ascontiguousarray(w[r * c:r * (c + 1)])
        maps.append(m)
    return maps


def host_prep_assemble(results):
    W = {}
    W["wgt"] = np.concatenate([r["wgt_s"] for r in results], axis=1)
    W["wut"] = np.concatenate([r["wut_s"] for r in results], axis=1)
    W["wdt"] = np.concatenate([r["wdt_s"] for r in results], axis=0)
    aw = np.concatenate([r["awt_s"] for r in results], axis=2)
    for j, nm in enumerate(["l0_wqt", "l0_wkt", "l0_wvt", "l0_wot",
                            "l1_wqt", "l1_wkt", "l1_wvt", "l1_wot"]):
        W[nm] = aw[j]
    l1_names = ["down_w", "up_w", "router_w", "shared_w1", "shared_w2", "core_w"] + \
               [f"e{e}_w{j}" for e in range(8) for j in (1, 2)]
    for nm in l1_names:
        W["l1_" + nm] = np.concatenate([r["o_" + nm] for r in results], axis=0)
    return W


def tile_lhsT(w):
    i, o = w.shape
    sw = 128 if o >= 128 else o
    return np.ascontiguousarray(w.reshape(i, o // sw, sw).transpose(1, 0, 2))





class WStream:

    def __init__(self, ctx, name, dom, nbuf, shape, dt=BF16):
        self.ctx = ctx
        self.name = name
        self.dom = dom
        self.nbuf = nbuf
        self.bufs = [ctx.sb(f"{name}{i}", shape, dt) for i in range(nbuf)]
        self.keys = [f"{name}{i}" for i in range(nbuf)]
        self.i = 0

    def load(self, dram_ap, tag=""):
        b = self.i % self.nbuf
        self.i += 1
        buf, key = self.bufs[b], self.keys[b]
        shp = [int(s) for s in dram_ap.shape]
        if len(shp) == 2:
            out_ap = buf[:, 0:shp[1]]
        elif len(shp) == 3:
            out_ap = buf[:, 0:shp[1] * shp[2]].rearrange("p (a b) -> p a b", b=shp[2])
        else:
            raise ValueError(shp)
        self.ctx.p.add(self.dom, (lambda o=out_ap, d=dram_ap: (lambda e: e.dma_start(out=o, in_=d)))(),
                       writes=[key], tag=f"{self.name}_ld{self.i}_{tag}")
        return buf, key


def emit_mm_group(ctx, psum_ap, psum_key, pairs, tag):
    p = ctx.p
    n = len(pairs)
    for i, (l, r, reads) in enumerate(pairs):
        p.add("pe", (lambda l=l, r=r, i=i, n=n, ps=psum_ap:
                     (lambda e: e.matmul(ps, l, r, start=(i == 0), stop=(i == n - 1))))(),
              reads=list(reads), writes=[psum_key], inc=(i == n - 1), tag=f"{tag}_mm{i}")


def emit_ln(ctx, tag, provider, nt, tpc, consts, pf, pfk, sinks):
    p = ctx.p
    u = ctx.uid()
    oc_ap, oc_k = consts["ones_col"]
    or_ap, or_k = consts["ones_row"]
    xc, xc_k = consts["xc"]
    sq, sq_k = consts["sq"]
    m_row, m_k = consts["m_row"]
    r_row, r_k = consts["r_row"]
    mr_row, mr_k = consts["mr_row"]
    nh = tpc // 512
    assert nh == 2
    sumbank = {0: 0, 1: 5}
    for i in range(nt):
        xap, xkeys = provider(i, 0)
        for th in range(nh):
            b = sumbank[th]
            p.add("pe", (lambda xap=xap, th=th, i=i, b=b: (lambda e: e.matmul(
                pf[b][0:1, 0:512], oc_ap, xap[:, 512 * th:512 * (th + 1)],
                start=(i == 0), stop=(i == nt - 1))))(),
                reads=list(xkeys) + [oc_k], writes=[pfk[b]], inc=True,
                tag=f"ln{u}_sum{th}_{i}")
    for th in range(nh):
        b = sumbank[th]
        p.add("dve", (lambda th=th, b=b: (lambda e: e.tensor_scalar(
            m_row[0:1, 512 * th:512 * (th + 1)], pf[b][0:1, 0:512], 1.0 / D, None, ALU.mult)))(),
            reads=[pfk[b]], writes=[m_k], tag=f"ln{u}_m{th}")
    for th in range(nh):
        bank, bkey = (pf[1], pfk[1]) if th == 0 else (pf[2], pfk[2])
        p.add("pe", (lambda th=th, bank=bank: (lambda e: e.matmul(
            bank[:, 0:512], or_ap, m_row[0:1, 512 * th:512 * (th + 1)], start=True, stop=True)))(),
            reads=[or_k, m_k], writes=[bkey], tag=f"ln{u}_mb{th}")
    vbank = {0: 3, 1: 0}
    for i in range(nt):
        xap, xkeys = provider(i, 1)
        for th in range(nh):
            mb, mbk = (pf[1], pfk[1]) if th == 0 else (pf[2], pfk[2])
            vb, vbk2 = pf[vbank[th]], pfk[vbank[th]]
            p.add("dve", (lambda xap=xap, th=th, mb=mb: (lambda e: e.tensor_tensor(
                xc[:, 0:512], xap[:, 512 * th:512 * (th + 1)], mb[:, 0:512], ALU.subtract)))(),
                reads=list(xkeys) + [mbk], writes=[xc_k], tag=f"ln{u}_xc{th}_{i}")
            p.add("act", (lambda: (lambda e: e.activation(sq[:, 0:512], xc[:, 0:512], AF.Square)))(),
                  reads=[xc_k], writes=[sq_k], tag=f"ln{u}_sq{th}_{i}")
            p.add("pe", (lambda i=i, th=th, vb=vb: (lambda e: e.matmul(
                vb[0:1, 0:512], oc_ap, sq[:, 0:512], start=(i == 0), stop=(i == nt - 1))))(),
                reads=[sq_k, oc_k], writes=[vbk2], inc=True, tag=f"ln{u}_vs{th}_{i}")
    for th in range(nh):
        vb, vbk2 = pf[vbank[th]], pfk[vbank[th]]
        p.add("dve", (lambda th=th, vb=vb: (lambda e: e.tensor_scalar(
            r_row[0:1, 512 * th:512 * (th + 1)], vb[0:1, 0:512], 1.0 / D, EPS_LN, ALU.mult, ALU.add)))(),
            reads=[vbk2], writes=[r_k], tag=f"ln{u}_v{th}")
        p.add("act", (lambda th=th: (lambda e: e.activation(
            r_row[0:1, 512 * th:512 * (th + 1)], r_row[0:1, 512 * th:512 * (th + 1)], AF.Ln)))(),
            reads=[r_k], writes=[r_k], tag=f"ln{u}_ln{th}")
        p.add("act", (lambda th=th: (lambda e: e.activation(
            r_row[0:1, 512 * th:512 * (th + 1)], r_row[0:1, 512 * th:512 * (th + 1)], AF.Exp, scale=-0.5)))(),
            reads=[r_k], writes=[r_k], tag=f"ln{u}_ex{th}")
    p.add("dve", lambda e: e.tensor_mul(mr_row[0:1, 0:tpc], m_row[0:1, 0:tpc], r_row[0:1, 0:tpc]),
          reads=[m_k, r_k], writes=[mr_k], tag=f"ln{u}_mr")
    bmap = {}
    for th in range(nh):
        rb, rbk = (pf[1], pfk[1]) if th == 0 else (pf[2], pfk[2])
        mb2, mb2k = (pf[3], pfk[3]) if th == 0 else (pf[4], pfk[4])
        p.add("pe", (lambda th=th, rb=rb: (lambda e: e.matmul(
            rb[:, 0:512], or_ap, r_row[0:1, 512 * th:512 * (th + 1)], start=True, stop=True)))(),
            reads=[or_k, r_k], writes=[rbk], tag=f"ln{u}_rb{th}")
        p.add("pe", (lambda th=th, mb2=mb2: (lambda e: e.matmul(
            mb2[:, 0:512], or_ap, mr_row[0:1, 512 * th:512 * (th + 1)], start=True, stop=True)))(),
            reads=[or_k, mr_k], writes=[mb2k], tag=f"ln{u}_mrb{th}")
        bmap[th] = (rb, rbk, mb2, mb2k)
    for i in range(nt):
        xap, xkeys = provider(i, 2)
        for th in range(nh):
            rb, rbk, mb2, mb2k = bmap[th]
            t1, t1k = consts["t1"]
            p.add("dve", (lambda xap=xap, th=th, rb=rb, t1=t1: (lambda e: e.tensor_tensor(
                t1[:, 0:512], xap[:, 512 * th:512 * (th + 1)], rb[:, 0:512], ALU.mult)))(),
                reads=list(xkeys) + [rbk], writes=[t1k], tag=f"ln{u}_t1_{i}_{th}")
            sinks(i, th, t1, t1k, mb2, mb2k)


def emit_ffn0(ctx, xn, xnk, inter, interk, acc_slot, gels, pf, pfk,
              ws_g, ws_u, ws_d, wg_t, wu_t, wd_t, tagp, th_list=(0, 1)):
    p = ctx.p
    for i in range(NT):
        for th in th_list:
            aap, akey = acc_slot(i, th)
            p.add("dve", (lambda aap=aap: (lambda e: e.memset(aap, 0.0)))(),
                  writes=[akey], tag=f"{tagp}accz{i}_{th}")
    for ch in range(16):
        for m in range(8):
            gm = 8 * ch + m
            gbuf, gkey = ws_g.load(wg_t[gm].rearrange("(kt p) n -> p kt n", p=128), tag=f"{tagp}g{gm}")
            ubuf, ukey = ws_u.load(wu_t[gm].rearrange("(kt p) n -> p kt n", p=128), tag=f"{tagp}u{gm}")
            for th in th_list:
                gps, gpsk = pf[(2 * ((m + th) % 2)) % 4], pfk[(2 * ((m + th) % 2)) % 4]
                ups, upsk = pf[(2 * ((m + th) % 2) + 1) % 4], pfk[(2 * ((m + th) % 2) + 1) % 4]
                pairs_g = [(gbuf[:, 128 * k:128 * (k + 1)],
                            xn[k][:, 512 * th:512 * (th + 1)], [gkey, xnk[k]])
                           for k in range(NT)]
                pairs_u = [(ubuf[:, 128 * k:128 * (k + 1)],
                            xn[k][:, 512 * th:512 * (th + 1)], [ukey, xnk[k]])
                           for k in range(NT)]
                emit_mm_group(ctx, gps[:, 0:512], gpsk, pairs_g, f"{tagp}gmm{gm}_{th}")
                emit_mm_group(ctx, ups[:, 0:512], upsk, pairs_u, f"{tagp}umm{gm}_{th}")
                gl, glk = gels[(2 * m + th) % len(gels)]
                p.add("act", (lambda gps=gps, gl=gl: (lambda e: e.activation(
                    gl[:, 0:512], gps[:, 0:512], AF.Gelu_apprx_tanh)))(),
                    reads=[gpsk], writes=[glk], tag=f"{tagp}gelu{gm}_{th}")
                p.add("dve", (lambda m=m, th=th, ups=ups, gl=gl: (lambda e: e.scalar_tensor_tensor(
                    inter[m][:, 512 * th:512 * (th + 1)], ups[:, 0:512], 1.0, gl[:, 0:512],
                    ALU.mult, ALU.mult)))(),
                    reads=[upsk, glk], writes=[interk[m]], tag=f"{tagp}inter{gm}_{th}")
        for hm in range(NT):
            dbuf, dkey = ws_d.load(wd_t[hm, 8 * ch:8 * (ch + 1)].rearrange("k p h -> p k h"),
                                   tag=f"{tagp}d{ch}_{hm}")
            for th in th_list:
                dps, dpsk = pf[4 + (th % 2)], pfk[4 + (th % 2)]
                aap, akey = acc_slot(hm, th)
                pairs = [(dbuf[:, 128 * k:128 * (k + 1)],
                          inter[k][:, 512 * th:512 * (th + 1)], [dkey, interk[k]])
                         for k in range(8)]
                emit_mm_group(ctx, dps[:, 0:512], dpsk, pairs, f"{tagp}dmm{ch}_{hm}_{th}")
                p.add("dve", (lambda aap=aap, dps=dps: (lambda e: e.tensor_tensor(
                    aap, aap, dps[:, 0:512], ALU.add)))(),
                    reads=[dpsk, akey], writes=[akey], tag=f"{tagp}dacc{ch}_{hm}_{th}")




def build_A(core):
    ctx = Ctx("A")
    p = ctx.p
    for d, b in [("qa", "sync"), ("qc", "sync"), ("qb", "scalar"), ("qx", "scalar"),
                 ("qr", "sync"), ("qo", "gpsimd")]:
        p.dma_domain(d, b)

    xT = ctx.inp("xT", [D, TPC])
    wg_t = ctx.inp("wg_t", [128, D, 128], BF16)
    wu_t = ctx.inp("wu_t", [128, D, 128], BF16)
    wd_t = ctx.inp("wd_t", [16, 128, 128, 128], BF16)
    wq_t = ctx.inp("wq_t", [16, D, 128], BF16)
    wk_t = ctx.inp("wk_t", [16, D, 128], BF16)
    wv_t = ctx.inp("wv_t", [16, D, 128], BF16)
    bq = ctx.inp("bq", [128, 16])
    bk = ctx.inp("bk", [128, 16])
    bv = ctx.inp("bv", [128, 16])
    decay = ctx.inp("decay", [1, TPC])
    ident_in = ctx.inp("ident_in", [128, 128])

    x1T_o = ctx.out("x1T", [D, TPC])
    qt_o = ctx.out("qt", [D, TPC], BF16)
    knat_o = ctx.out("knat", [NH, TPC, DH], BF16)
    vnat_o = ctx.out("vnat", [NH, TPC, DH], BF16)

    acc = [ctx.sb(f"acc{i}", [128, TPC]) for i in range(NT)]
    acck = [f"acc{i}" for i in range(NT)]
    xn = [ctx.sb(f"xn{i}", [128, TPC], BF16) for i in range(NT)]
    xnk = [f"xn{i}" for i in range(NT)]
    inter = [ctx.sb(f"inter{i}", [128, TPC], BF16) for i in range(8)]
    interk = [f"inter{i}" for i in range(8)]
    xbuf = [ctx.sb(f"xb{i}", [128, TPC]) for i in range(3)]
    xbk = [f"xb{i}" for i in range(3)]
    ones_col = ctx.sb("ones_col", [128, 1])
    ones_row = ctx.sb("ones_row", [1, 128])
    xc = ctx.sb("xc", [128, 512])
    sq = ctx.sb("sq", [128, 512])
    t1s = ctx.sb("t1s", [128, 512])
    gel = ctx.sb("gel", [128, 512])
    gel2 = ctx.sb("gel2", [128, 512])
    m_row = ctx.sb("m_row", [1, TPC])
    r_row = ctx.sb("r_row", [1, TPC])
    mr_row = ctx.sb("mr_row", [1, TPC])
    sbias = {nm: ctx.sb("s" + nm, [128, 16]) for nm in ("bq", "bk", "bv")}
    decay_b = ctx.sb("decay_b", [128, TPC])
    ident = ctx.sb("ident", [128, 128])
    ident_bf = ctx.sb("ident_bf", [128, 128], BF16)
    stage = [ctx.sb(f"stage{i}", [128, TPC], BF16) for i in range(2)]
    stagek = [f"stage{i}" for i in range(2)]
    pf = [ctx.psum(f"pf{i}", [128, 512]) for i in range(6)]
    pfk = [f"pf{i}" for i in range(6)]
    pb = [ctx.psum(f"pbb{i}", [128, 1024], BF16) for i in range(2)]
    pbk = ["pbb0", "pbb1"]

    ws_g = WStream(ctx, "wsg", "qa", 2, [128, 2048])
    ws_u = WStream(ctx, "wsu", "qb", 2, [128, 2048])
    ws_d = WStream(ctx, "wsd", "qc", 2, [128, 1024])

    emit_loads(ctx, "qx", [(sbias["bq"][:], bq, "sbq"), (sbias["bk"][:], bk, "sbk"),
                           (sbias["bv"][:], bv, "sbv"), (ident[:], ident_in, "ident")],
               tag="consts")
    p.add("dve", lambda e: e.memset(ones_col[:], 1.0), writes=["ones_col"], tag="oc")
    ones_col_bf = ctx.sb("ones_col_bf", [128, 1], BF16)
    p.add("dve", lambda e: e.memset(ones_col_bf[:], 1.0), writes=["ones_col_bf"], tag="ocb")
    p.add("dve", lambda e: e.memset(ones_row[:], 1.0), writes=["ones_row"], tag="or")
    p.add("dve", lambda e: e.tensor_copy(ident_bf[:], ident[:]), reads=["ident"],
          writes=["ident_bf"], tag="identc")
    dconst = ctx.sb("dconst", [1, TPC])
    emit_loads(ctx, "qx", [(dconst[:], decay, "dconst")], tag="decayld")
    for th in range(2):
        p.add("pe", (lambda th=th: (lambda e: e.matmul(
            pf[5][:, 0:512], ones_row[0:1, :], dconst[0:1, 512 * th:512 * (th + 1)],
            start=True, stop=True)))(),
            reads=["ones_row", "dconst"], writes=[pfk[5]], tag=f"db{th}")
        p.add("dve", (lambda th=th: (lambda e: e.tensor_copy(
            decay_b[:, 512 * th:512 * (th + 1)], pf[5][:, 0:512])))(),
            reads=[pfk[5]], writes=["decay_b"], tag=f"dbe{th}")

    ln_consts = {"ones_col": (ones_col[:, 0:1], "ones_col"),
                 "ones_row": (ones_row[0:1, :], "ones_row"),
                 "xc": (xc, "xc"), "sq": (sq, "sq"), "t1": (t1s, "t1s"),
                 "m_row": (m_row, "m_row"), "r_row": (r_row, "r_row"),
                 "mr_row": (mr_row, "mr_row")}

    xprov_state = {"n": 0}

    def xprov(i, passno):
        b = xprov_state["n"] % 3
        xprov_state["n"] += 1
        buf, key = xbuf[b], xbk[b]
        p.add(["qx", "qr"][xprov_state["n"] % 2], (lambda buf=buf, i=i: (lambda e: e.dma_start(
            out=buf[:], in_=xT[128 * i:128 * (i + 1), :])))(),
            writes=[key], tag=f"xld_{passno}_{i}")
        return buf[:], [key]

    def ln1_sink(i, th, t1, t1k, mb2, mb2k):
        p.add("dve", (lambda i=i, th=th, t1=t1, mb2=mb2: (lambda e: e.tensor_tensor(
            xn[i][:, 512 * th:512 * (th + 1)], t1[:, 0:512], mb2[:, 0:512], ALU.subtract)))(),
            reads=[t1k, mb2k], writes=[xnk[i]], tag=f"ln1sink_{i}_{th}")

    emit_ln(ctx, "ln1", xprov, NT, TPC, ln_consts, pf, pfk, ln1_sink)

    if DEBUG_A[0]:
        xn_dbg = ctx.out("xn_dbg", [D, TPC], BF16)
        for i in range(NT):
            p.add("qo", (lambda i=i: (lambda e: e.dma_start(
                out=xn_dbg[128 * i:128 * (i + 1), :], in_=xn[i][:])))(),
                reads=[xnk[i]], tag=f"dbgxn{i}")
        rows_dbg = ctx.out("rows_dbg", [3, TPC], F32)
        for j, (row, rk) in enumerate([(m_row, "m_row"), (r_row, "r_row"), (mr_row, "mr_row")]):
            p.add("qo", (lambda j=j, row=row: (lambda e: e.dma_start(
                out=rows_dbg[j:j + 1, :], in_=row[0:1, :])))(),
                reads=[rk], tag=f"dbgrow{j}")

    emit_ffn0(ctx, xn, xnk, inter, interk,
              lambda i, th: (acc[i][:, 512 * th:512 * (th + 1)], acck[i]),
              [(gel, "gel"), (gel2, "gel2")], pf, pfk, ws_g, ws_u, ws_d,
              wg_t, wu_t, wd_t, "f0")

    if DEBUG_A[0]:
        ffn_dbg = ctx.out("ffn_dbg", [D, TPC], F32)
        for i in range(NT):
            p.add("qo", (lambda i=i: (lambda e: e.dma_start(
                out=ffn_dbg[128 * i:128 * (i + 1), :], in_=acc[i][:])))(),
                reads=[acck[i]], tag=f"dbgffn{i}")

    for i in range(NT):
        xap, xkeys = xprov(i, 3)
        p.add("dve", (lambda i=i, xap=xap: (lambda e: e.scalar_tensor_tensor(
            acc[i][:], acc[i][:], 0.5, xap, ALU.mult, ALU.add)))(),
            reads=[acck[i]] + list(xkeys), writes=[acck[i]], tag=f"resid1_{i}")
        p.add("qo", (lambda i=i: (lambda e: e.dma_start(
            out=x1T_o[128 * i:128 * (i + 1), :], in_=acc[i][:])))(),
            reads=[acck[i]], tag=f"x1st_{i}")

    def x1prov(i, passno):
        return acc[i][:], [acck[i]]

    emit_ln(ctx, "ln2", x1prov, NT, TPC, ln_consts, pf, pfk, ln1_sink)

    for mat, (wt, bias_t, bkey2, kind) in enumerate([
            (wq_t, sbias["bq"], "sbq", "q"), (wk_t, sbias["bk"], "sbk", "k"),
            (wv_t, sbias["bv"], "sbv", "v")]):
        for m in range(NT):
            wbuf, wkey = ws_g.load(wt[m].rearrange("(kt p) n -> p kt n", p=128), tag=f"{kind}w{m}")
            st = stage[m % 2]
            stk = stagek[m % 2]
            for th in range(2):
                ps, psk = pf[th], pfk[th]
                pairs = [(wbuf[:, 128 * k:128 * (k + 1)],
                          xn[k][:, 512 * th:512 * (th + 1)], [wkey, xnk[k]]) for k in range(NT)]
                emit_mm_group(ctx, ps[:, 0:512], psk, pairs, f"{kind}mm{m}_{th}")
                if kind == "v":
                    p.add("dve", (lambda m=m, th=th, ps=ps, bias_t=bias_t, st=st:
                                  (lambda e: e.tensor_scalar(
                                      st[:, 512 * th:512 * (th + 1)], ps[:, 0:512],
                                      bias_t[:, m:m + 1], None, ALU.add)))(),
                          reads=[psk, bkey2], writes=[stk], tag=f"vep{m}_{th}")
                else:
                    p.add("act", (lambda m=m, ps=ps, bias_t=bias_t:
                                  (lambda e: e.activation(gel[:, 0:512], ps[:, 0:512],
                                                          AF.Relu, bias=bias_t[:, m:m + 1])))(),
                          reads=[psk, bkey2], writes=["gel"], tag=f"{kind}relu{m}_{th}")
                    p.add("dve", (lambda m=m, th=th, st=st: (lambda e: e.tensor_tensor(
                        st[:, 512 * th:512 * (th + 1)], gel[:, 0:512],
                        decay_b[:, 512 * th:512 * (th + 1)], ALU.mult)))(),
                        reads=["gel", "decay_b"], writes=[stk], tag=f"{kind}dec{m}_{th}")
            if kind == "q":
                p.add("qo", (lambda m=m, st=st: (lambda e: e.dma_start(
                    out=qt_o[128 * m:128 * (m + 1), :], in_=st[:])))(),
                    reads=[stk], tag=f"qst{m}")
            else:
                dst = knat_o if kind == "k" else vnat_o
                for j in range(8):
                    bank, bankk = pb[j % 2], pbk[j % 2]
                    p.add("pe", (lambda st=st, j=j, bank=bank: (lambda e: e.transpose(
                        bank[:, 0:128], st[:, 128 * j:128 * (j + 1)], ident_bf[:])))(),
                        reads=[stk, "ident_bf"], writes=[bankk], tag=f"{kind}tr{m}_{j}")
                    p.add("dve", (lambda j=j, bank=bank, st2=stage[1 - (m % 2)]: (lambda e: e.tensor_copy(
                        st2[:, 128 * j:128 * (j + 1)], bank[:, 0:128])))(),
                        reads=[bankk], writes=[stagek[1 - (m % 2)]], tag=f"{kind}te{m}_{j}")
                p.add("qo", (lambda m=m, dst=dst, st2=stage[1 - (m % 2)]: (lambda e: e.dma_start(
                    out=dst[m].rearrange("(j p) n -> p j n", p=128),
                    in_=st2[:, 0:1024].rearrange("p (j n) -> p j n", n=128))))(),
                    reads=[stagek[1 - (m % 2)]], tag=f"{kind}natst{m}")

    ctx.finish()
    return ctx





def emit_moe(ctx, E, resid_dram, resid_key, out_drams, out_keys, tagp):
    p = ctx.p
    BFA, BFAk = E["BFA"], E["BFAk"]
    BFB, BFBk = E["BFB"], E["BFBk"]
    BFC, BFCk = E["BFC"], E["BFCk"]
    AC, ACk = E["AC"], E["ACk"]
    pf, pfk = E["pf"], E["pfk"]
    ws_g, ws_d, ws_k = E["ws_g"], E["ws_d"], E["ws_k"]
    down_t, router_t, sw1_t, sw2_t = E["down_t"], E["router_t"], E["sw1_t"], E["sw2_t"]
    core_t, up_t, ew1_t, ew2_t = E["core_t"], E["up_t"], E["ew1_t"], E["ew2_t"]
    sb_small, rb8, oh16, ident = E["sb_small"], E["rb8"], E["oh16"], E["ident"]
    gel, t2s, csb, Ccol, negc8, b1g = E["gel"], E["t2s"], E["csb"], E["Ccol"], E["negc8"], E["b1g"]
    t2s2, t2s2k = E["t2s2"]
    t2l = [(t2s, "t2s"), (t2s2, t2s2k)]
    W8, m8t, i8t, i8f, gcol, probs, srow = E["W8"], E["m8t"], E["i8t"], E["i8f"], E["gcol"], E["probs"], E["srow"]
    wrows_bf, wrows_f = E["wrows_bf"], E["wrows_f"]
    lgT = E["lgT"]
    ones_row = E["ones_row"]
    bsum = E["bsum"]
    dram_provider = E["dram_provider"]
    u = ctx.uid()
    p.add("act", lambda e: e.activation(b1g[:], sb_small["b1cat"][:], AF.Gelu),
          reads=["s_b1cat"], writes=["b1g"], tag="b1gelu")
    for e_ix in range(8):
        for l in range(8):
            w2b, w2k = ws_k.load(ew2_t[e_ix][l].rearrange("(kt p) n -> p kt n", p=128),
                                 tag=f"c2_{e_ix}_{l}")
            pairs = [(w2b[:, 128 * k:128 * (k + 1)],
                      b1g[:, HE_OFF[e_ix] + k:HE_OFF[e_ix] + k + 1], [w2k, "b1g"])
                     for k in range(HE_TILES[e_ix])]
            emit_mm_group(ctx, pf[0][:, 0:1], pfk[0], pairs, f"ce{e_ix}_{l}")
            p.add("dve", (lambda e_ix=e_ix, l=l: (lambda e: e.tensor_copy(
                csb[:, 8 * l + e_ix:8 * l + e_ix + 1], pf[0][:, 0:1])))(),
                reads=[pfk[0]], writes=["csb"], tag=f"ceev{e_ix}_{l}")
    for l in range(8):
        p.add("dve", (lambda l=l: (lambda e: e.reduce_sum(
            Ccol[:, l:l + 1], csb[:, 8 * l:8 * (l + 1)], axis=X)))(),
            reads=["csb"], writes=["Ccol"], tag=f"cc{l}")
        p.add("dve", (lambda l=l: (lambda e: e.reduce_sum(
            srow[:, 0:1], sb_small["b2cat"][:, l:l + 1].broadcast_to([128, 1]) if False else
            sb_small["b2cat"][:].rearrange("p (e l2) -> p e l2", l2=8)[:, :, l], axis=X)))(),
            reads=["s_b2cat"], writes=["srow"], tag=f"b2s{l}")
        p.add("dve", (lambda l=l: (lambda e: e.tensor_add(
            Ccol[:, l:l + 1], Ccol[:, l:l + 1], srow[:, 0:1])))(),
            reads=["Ccol", "srow"], writes=["Ccol"], tag=f"ccb{l}")
        p.add("pe", (lambda l=l: (lambda e: e.transpose(
            pf[1][0:8, 0:128], csb[:, 8 * l:8 * (l + 1)], ident[:])))(),
            reads=["csb", "ident"], writes=[pfk[1]], tag=f"ctr{l}")
        p.add("dve", (lambda l=l: (lambda e: e.tensor_scalar(
            negc8[0:8, 128 * l:128 * (l + 1)], pf[1][0:8, 0:128], -1.0, None, ALU.mult)))(),
            reads=[pfk[1]], writes=["negc8"], tag=f"cneg{l}")
    rbuf, rbk = ws_d.load(router_t[0].rearrange("(kt p) n -> p kt n", p=128), tag="rt")
    for th in range(2):
        pairs = [(rbuf[:, 8 * k:8 * (k + 1)], BFA[k][:, 512 * th:512 * (th + 1)],
                  [rbk, BFAk[k]]) for k in range(NT)]
        emit_mm_group(ctx, pf[2][0:8, 0:512], pfk[2], pairs, f"rmm{th}")
        p.add("dve", (lambda th=th: (lambda e: e.tensor_scalar(
            lgT[0:8, 512 * th:512 * (th + 1)], pf[2][0:8, 0:512], rb8[0:8, 0:1], None, ALU.add)))(),
            reads=[pfk[2], "rb8"], writes=["zrow"], tag=f"rev{th}")
    for tt in range(8):
        p.add("pe", (lambda tt=tt: (lambda e: e.transpose(
            pf[3][:, 0:8], lgT[0:8, 128 * tt:128 * (tt + 1)], ident[0:8, 0:8])))(),
            reads=["zrow", "ident"], writes=[pfk[3]], tag=f"lgtr{tt}")
        p.add("dve", lambda e: e.reduce_max(srow[:, 0:1], pf[3][:, 0:8], axis=X),
              reads=[pfk[3]], writes=["srow"], tag=f"lgmax{tt}")
        p.add("dve", lambda e: e.tensor_scalar(srow[:, 0:1], srow[:, 0:1], -1.0, None, ALU.mult),
              reads=["srow"], writes=["srow"], tag=f"lgneg{tt}")
        p.add("act", lambda e: e.activation(probs[:], pf[3][:, 0:8], AF.Exp,
                                            bias=srow[:, 0:1], accum_out=gcol[:, 3:4]),
              reads=[pfk[3], "srow"], writes=["probs", "gcol"], tag=f"lgexp{tt}")
        p.add("dve", lambda e: e.reciprocal(gcol[:, 3:4], gcol[:, 3:4]),
              reads=["gcol"], writes=["gcol"], tag=f"lgrec{tt}")
        p.add("dve", lambda e: e.tensor_scalar(probs[:], probs[:], gcol[:, 3:4], None, ALU.mult),
              reads=["probs", "gcol"], writes=["probs"], tag=f"lgnorm{tt}")
        p.add("dve", lambda e: e.max(m8t[:], probs[:]), reads=["probs"], writes=["m8t"],
              tag=f"max8{tt}")
        p.add("dve", lambda e: e.max_index(i8t[:], m8t[:], probs[:]),
              reads=["probs", "m8t"], writes=["i8t"], tag=f"mi8{tt}")
        p.add("dve", lambda e: e.tensor_copy(i8f[:, 0:2], i8t[:, 0:2]),
              reads=["i8t"], writes=["i8f"], tag=f"i8c{tt}")
        p.add("dve", lambda e: e.tensor_sub(gcol[:, 0:1], m8t[:, 0:1], m8t[:, 1:2]),
              reads=["m8t"], writes=["gcol"], tag=f"gd{tt}")
        p.add("act", lambda e: e.activation(gcol[:, 1:2], gcol[:, 0:1], AF.Sigmoid),
              reads=["gcol"], writes=["gcol"], tag=f"g1{tt}")
        p.add("dve", lambda e: e.tensor_scalar(gcol[:, 2:3], gcol[:, 1:2], -1.0, 1.0,
                                               ALU.mult, ALU.add),
              reads=["gcol"], writes=["gcol"], tag=f"g2{tt}")
        for e_ix in range(8):
            p.add("dve", (lambda e_ix=e_ix: (lambda e: e.tensor_scalar(
                W8[:, e_ix:e_ix + 1], i8f[:, 0:1], float(e_ix), None, ALU.is_equal)))(),
                reads=["i8f"], writes=["W8"], tag=f"we1_{tt}_{e_ix}")
            p.add("dve", (lambda e_ix=e_ix: (lambda e: e.tensor_mul(
                W8[:, e_ix:e_ix + 1], W8[:, e_ix:e_ix + 1], gcol[:, 1:2])))(),
                reads=["W8", "gcol"], writes=["W8"], tag=f"we2_{tt}_{e_ix}")
            p.add("dve", (lambda e_ix=e_ix: (lambda e: e.tensor_scalar(
                srow[:, 0:1], i8f[:, 1:2], float(e_ix), None, ALU.is_equal)))(),
                reads=["i8f"], writes=["srow"], tag=f"we3_{tt}_{e_ix}")
            p.add("dve", (lambda e_ix=e_ix: (lambda e: e.tensor_mul(
                srow[:, 0:1], srow[:, 0:1], gcol[:, 2:3])))(),
                reads=["srow", "gcol"], writes=["srow"], tag=f"we4_{tt}_{e_ix}")
            p.add("dve", (lambda e_ix=e_ix: (lambda e: e.tensor_add(
                W8[:, e_ix:e_ix + 1], W8[:, e_ix:e_ix + 1], srow[:, 0:1])))(),
                reads=["W8", "srow"], writes=["W8"], tag=f"we5_{tt}_{e_ix}")
        p.add("pe", (lambda tt=tt: (lambda e: e.transpose(
            pf[4][0:8, 0:128], W8[:], ident[:])))(),
            reads=["W8", "ident"], writes=[pfk[4]], tag=f"wtr{tt}")
        p.add("dve", (lambda tt=tt: (lambda e: e.tensor_copy(
            wrows_bf[0:8, 128 * tt:128 * (tt + 1)], pf[4][0:8, 0:128])))(),
            reads=[pfk[4]], writes=["wrows_bf"], tag=f"wbf{tt}")
        p.add("dve", (lambda tt=tt: (lambda e: e.tensor_copy(
            wrows_f[0:8, 128 * tt:128 * (tt + 1)], pf[4][0:8, 0:128])))(),
            reads=[pfk[4]], writes=["wrows_f"], tag=f"wf{tt}")
    for m in range(8):
        dbuf, dk = ws_g.load(down_t[m].rearrange("(kt p) n -> p kt n", p=128), tag=f"dn{m}")
        for th in range(2):
            ps, psk = pf[th], pfk[th]
            pairs = [(dbuf[:, 128 * k:128 * (k + 1)], BFA[k][:, 512 * th:512 * (th + 1)],
                      [dk, BFAk[k]]) for k in range(NT)]
            emit_mm_group(ctx, ps[:, 0:512], psk, pairs, f"dnmm{m}_{th}")
            p.add("act", (lambda m=m, th=th, ps=ps: (lambda e: e.activation(
                BFB[m][:, 512 * th:512 * (th + 1)], ps[:, 0:512], AF.Gelu,
                bias=sb_small["down_b"][:, m:m + 1])))(),
                reads=[psk, "s_down_b"], writes=[BFBk[m]], tag=f"gep{m}_{th}")
    for l in range(8):
        for th in range(2):
            p.add("pe", (lambda l=l, th=th: (lambda e: e.matmul(
                pf[2][:, 0:512], negc8[0:8, 128 * l:128 * (l + 1)],
                wrows_bf[0:8, 512 * th:512 * (th + 1)], start=True, stop=True)))(),
                reads=["negc8", "wrows_bf"], writes=[pfk[2]], tag=f"cc{l}_{th}")
            p.add("dve", (lambda l=l, th=th: (lambda e: e.tensor_scalar(
                AC[l][:, 512 * th:512 * (th + 1)], pf[2][:, 0:512], Ccol[:, l:l + 1],
                None, ALU.add)))(),
                reads=[pfk[2], "Ccol"], writes=[ACk[l]], tag=f"ccev{l}_{th}")
    for e_ix in range(8):
        het = HE_TILES[e_ix]
        for th in range(2):
            p.add("pe", (lambda e_ix=e_ix, th=th: (lambda e: e.matmul(
                pf[5][:, 0:512], oh16[0:8, 128 * e_ix:128 * (e_ix + 1)],
                wrows_f[0:8, 512 * th:512 * (th + 1)], start=True, stop=True)))(),
                reads=["oh16", "wrows_f"], writes=[pfk[5]], tag=f"web{e_ix}_{th}")
            p.add("dve", (lambda: (lambda e: e.tensor_copy(gel[:, 0:512], pf[5][:, 0:512])))(),
                  reads=[pfk[5]], writes=["gel"], tag=f"webe{e_ix}_{th}")
            for m in range(het):
                w1b, w1k = ws_g.load(ew1_t[e_ix][m].rearrange("(kt p) n -> p kt n", p=128),
                                     tag=f"e{e_ix}w1_{m}_{th}")
                ps, psk = pf[m % 2], pfk[m % 2]
                pairs = [(w1b[:, 128 * k:128 * (k + 1)], BFB[k][:, 512 * th:512 * (th + 1)],
                          [w1k, BFBk[k]]) for k in range(8)]
                emit_mm_group(ctx, ps[:, 0:512], psk, pairs, f"u{e_ix}_{m}_{th}")
                p.add("act", (lambda e_ix=e_ix, m=m, th=th, ps=ps: (lambda e: e.activation(
                    BFC[m // 2][:, 512 * (m % 2):512 * (m % 2) + 512], ps[:, 0:512],
                    AF.Gelu, bias=sb_small["b1cat"][:, HE_OFF[e_ix] + m:HE_OFF[e_ix] + m + 1])))(),
                    reads=[psk, "s_b1cat"], writes=[BFCk[m // 2]], tag=f"mid{e_ix}_{m}_{th}")
            for l in range(8):
                w2b, w2k = ws_k.load(ew2_t[e_ix][l].rearrange("(kt p) n -> p kt n", p=128),
                                     tag=f"e{e_ix}w2_{l}_{th}")
                ps, psk = pf[2 + (l % 2)], pfk[2 + (l % 2)]
                pairs = [(w2b[:, 128 * k:128 * (k + 1)],
                          BFC[k // 2][:, 512 * (k % 2):512 * (k % 2) + 512], [w2k, BFCk[k // 2]])
                         for k in range(het)]
                emit_mm_group(ctx, ps[:, 0:512], psk, pairs, f"y{e_ix}_{l}_{th}")
                tb, tbk = t2l[l % 2]
                p.add("dve", (lambda l=l, th=th, ps=ps, tb=tb: (lambda e: e.tensor_tensor(
                    tb[:, 0:512], ps[:, 0:512], gel[:, 0:512], ALU.mult)))(),
                    reads=[psk, "gel"], writes=[tbk], tag=f"yev{e_ix}_{l}_{th}")
                p.add("dve", (lambda l=l, th=th, tb=tb: (lambda e: e.tensor_tensor(
                    AC[l][:, 512 * th:512 * (th + 1)], AC[l][:, 512 * th:512 * (th + 1)],
                    tb[:, 0:512], ALU.add)))(),
                    reads=[tbk, ACk[l]], writes=[ACk[l]], tag=f"yacc{e_ix}_{l}_{th}")
    for th in range(2):
        for m in range(8):
            w1b, w1k = ws_g.load(sw1_t[m].rearrange("(kt p) n -> p kt n", p=128),
                                 tag=f"sw1_{m}_{th}")
            ps, psk = pf[m % 2], pfk[m % 2]
            pairs = [(w1b[:, 128 * k:128 * (k + 1)], BFB[k][:, 512 * th:512 * (th + 1)],
                      [w1k, BFBk[k]]) for k in range(8)]
            emit_mm_group(ctx, ps[:, 0:512], psk, pairs, f"sh1_{m}_{th}")
            p.add("act", (lambda m=m, th=th, ps=ps: (lambda e: e.activation(
                BFC[m // 2][:, 512 * (m % 2):512 * (m % 2) + 512], ps[:, 0:512],
                AF.Gelu, bias=sb_small["sb1"][:, m:m + 1])))(),
                reads=[psk, "s_sb1"], writes=[BFCk[m // 2]], tag=f"smid{m}_{th}")
        for l in range(8):
            w2b, w2k = ws_k.load(sw2_t[l].rearrange("(kt p) n -> p kt n", p=128),
                                 tag=f"sw2_{l}_{th}")
            ps, psk = pf[2 + (l % 2)], pfk[2 + (l % 2)]
            pairs = [(w2b[:, 128 * k:128 * (k + 1)],
                      BFC[k // 2][:, 512 * (k % 2):512 * (k % 2) + 512], [w2k, BFCk[k // 2]])
                     for k in range(8)]
            emit_mm_group(ctx, ps[:, 0:512], psk, pairs, f"sh2_{l}_{th}")
            tb, tbk = t2l[l % 2]
            p.add("dve", (lambda l=l, th=th, ps=ps, tb=tb: (lambda e: e.tensor_scalar(
                tb[:, 0:512], ps[:, 0:512], sb_small["sb2"][:, l:l + 1], 0.1,
                ALU.add, ALU.mult)))(),
                reads=[psk, "s_sb2"], writes=[tbk], tag=f"shev{l}_{th}")
            p.add("dve", (lambda l=l, th=th, tb=tb: (lambda e: e.tensor_tensor(
                AC[l][:, 512 * th:512 * (th + 1)], AC[l][:, 512 * th:512 * (th + 1)],
                tb[:, 0:512], ALU.add)))(),
                reads=[tbk, ACk[l]], writes=[ACk[l]], tag=f"shacc{l}_{th}")
    for l in range(8):
        p.add("dve", (lambda l=l: (lambda e: e.tensor_copy(BFB[l][:], AC[l][:])))(),
              reads=[ACk[l]], writes=[BFBk[l]], tag=f"acbf{l}")
    for i in range(NT):
        p.add("act", (lambda i=i: (lambda e: e.activation(BFC[i][:], BFA[i][:], AF.Gelu)))(),
              reads=[BFAk[i]], writes=[BFCk[i]], tag=f"gxn{i}")
    x3prov = dram_provider(resid_dram, f"{tagp}xr", resid_key)
    for m in range(NT):
        ub, uk = ws_k.load(up_t[m].rearrange("(kt p) n -> p kt n", p=128), tag=f"{tagp}up{m}")
        cb, ck = ws_g.load(core_t[m].rearrange("(kt p) n -> p kt n", p=128), tag=f"{tagp}cr{m}")
        xap, xkeys = x3prov(m, 9)
        for th in range(2):
            ps, psk = pf[th], pfk[th]
            pairs = [(ub[:, 128 * k:128 * (k + 1)], BFB[k][:, 512 * th:512 * (th + 1)],
                      [uk, BFBk[k]]) for k in range(8)]
            pairs += [(cb[:, 128 * k:128 * (k + 1)], BFC[k][:, 512 * th:512 * (th + 1)],
                       [ck, BFCk[k]]) for k in range(NT)]
            emit_mm_group(ctx, ps[:, 0:512], psk, pairs, f"{tagp}upcr{m}_{th}")
            tb, tbk = t2l[m % 2]
            p.add("dve", (lambda m=m, th=th, ps=ps, tb=tb: (lambda e: e.tensor_scalar(
                tb[:, 0:512], ps[:, 0:512], bsum[:, m:m + 1], 0.5, ALU.add, ALU.mult)))(),
                reads=[psk, "bsum"], writes=[tbk], tag=f"{tagp}upep{m}_{th}")
            p.add("dve", (lambda m=m, th=th, xap=xap, tb=tb: (lambda e: e.tensor_tensor(
                xap[:, 512 * th:512 * (th + 1)], xap[:, 512 * th:512 * (th + 1)],
                tb[:, 0:512], ALU.add)))(),
                reads=[tbk] + list(xkeys), writes=list(xkeys), tag=f"{tagp}upres{m}_{th}")
        for od, ok in zip(out_drams, out_keys):
            p.add("qo", (lambda m=m, xap=xap, od=od: (lambda e: e.dma_start(
                out=od[128 * m:128 * (m + 1), :], in_=xap)))(),
                reads=list(xkeys), writes=([ok] if ok else []), tag=f"{tagp}xout{m}_{ok}")




HE_TILES = [h // 128 for h in HID]
HE_OFF = [sum(HE_TILES[:e]) for e in range(8)]
L1SCALE = float(1.0 / np.sqrt(np.sqrt(128.0)))


def build_B():
    import os
    B_STOP = int(os.environ.get("B_STOP", "9"))
    ctx = Ctx("B")
    p = ctx.p
    for d, b in [("qa", "sync"), ("qc", "sync"), ("qb", "scalar"), ("qx", "scalar"),
                 ("qr", "sync"), ("qo", "gpsimd")]:
        p.dma_domain(d, b)

    x1T = ctx.inp("x1T", [D, TPC])
    q0t_in = ctx.inp("q0t", [D, TPC], BF16)
    k0nat = ctx.inp("k0nat", [NH, S, DH], BF16)
    v0nat = ctx.inp("v0nat", [NH, S, DH], BF16)
    wo0_t = ctx.inp("wo0_t", [16, D, 128], BF16)
    wg_t = ctx.inp("wg_t", [128, D, 128], BF16)
    wu_t = ctx.inp("wu_t", [128, D, 128], BF16)
    wd_t = ctx.inp("wd_t", [16, 128, 128, 128], BF16)
    down_t = ctx.inp("down_t", [8, D, 128], BF16)
    up_t = ctx.inp("up_t", [16, L1L, 128], BF16)
    router_t = ctx.inp("router_t", [1, D, 8], BF16)
    sw1_t = ctx.inp("sw1_t", [8, L1L, 128], BF16)
    sw2_t = ctx.inp("sw2_t", [8, L1L, 128], BF16)
    core_t = ctx.inp("core_t", [16, D, 128], BF16)
    ew1_t = [ctx.inp(f"e{e}w1_t", [HE_TILES[e], L1L, 128], BF16) for e in range(8)]
    ew2_t = [ctx.inp(f"e{e}w2_t", [8, HID[e], 128], BF16) for e in range(8)]
    wq1_t = ctx.inp("wq1_t", [16, D, 128], BF16)
    wk1_t = ctx.inp("wk1_t", [16, D, 128], BF16)
    wv1_t = ctx.inp("wv1_t", [16, D, 128], BF16)
    bo0 = ctx.inp("bo0", [128, 16])
    down_b = ctx.inp("down_b", [128, 8])
    b1cat = ctx.inp("b1cat", [128, 160])
    b2cat = ctx.inp("b2cat", [128, 64])
    sb1 = ctx.inp("sb1", [128, 8])
    sb2 = ctx.inp("sb2", [128, 8])
    bsum_in = ctx.inp("bsum_in", [128, 16])
    upb_in = ctx.inp("upb_in", [128, 16])
    coreb_in = ctx.inp("coreb_in", [128, 16])
    rb_in = ctx.inp("rb_in", [8, 1])
    nlc_in = ctx.inp("nlc_in", [8, 1])
    bq1 = ctx.inp("bq1", [128, 16])
    bk1 = ctx.inp("bk1", [128, 16])
    bv1 = ctx.inp("bv1", [128, 16])
    ident_in = ctx.inp("ident_in", [128, 128])
    oh16_in = ctx.inp("oh16_in", [16, 2048])

    x4T_o = ctx.out("x4T", [D, TPC])
    q1t_o = ctx.out("q1t", [D, TPC], BF16)
    k1t_o = ctx.out("k1t", [D, TPC], BF16)
    v1t_o = ctx.out("v1t", [D, TPC], BF16)

    xr2 = ctx.nc.dram_tensor("xr2", [D, TPC], F32, kind="Internal").ap()
    xr3 = ctx.nc.dram_tensor("xr3", [D, TPC], F32, kind="Internal").ap()
    xr4 = ctx.nc.dram_tensor("xr4", [D, TPC], F32, kind="Internal").ap()

    AC = [ctx.sb(f"AC{i}", [128, TPC]) for i in range(8)]
    ACk = [f"AC{i}" for i in range(8)]
    BFA = [ctx.sb(f"BFA{i}", [128, TPC], BF16) for i in range(NT)]
    BFAk = [f"BFA{i}" for i in range(NT)]
    BFB = [ctx.sb(f"BFB{i}", [128, TPC], BF16) for i in range(8)]
    BFBk = [f"BFB{i}" for i in range(8)]
    BFC = [ctx.sb(f"BFC{i}", [128, TPC], BF16) for i in range(NT)]
    BFCk = [f"BFC{i}" for i in range(NT)]
    RS = [ctx.sb(f"RS{i}", [128, TPC]) for i in range(2)]
    RSk = [f"RS{i}" for i in range(2)]
    kvs = ctx.sb("kvs", [128, 2048], BF16)
    bdiag = ctx.sb("bdiag", [128, 256], BF16)
    zrow = ctx.sb("zrow", [16, TPC])
    lgT = zrow
    wrows_bf = ctx.sb("wrows_bf", [8, TPC], BF16)
    wrows_f = ctx.sb("wrows_f", [8, TPC])
    csb = ctx.sb("csb", [128, 64])
    Ccol = ctx.sb("Ccol", [128, 8])
    negc8 = ctx.sb("negc8", [8, 1024], BF16)
    b1g = ctx.sb("b1g", [128, 160], BF16)
    W8 = ctx.sb("W8", [128, 8])
    m8t = ctx.sb("m8t", [128, 8])
    i8t = ctx.sb("i8t", [128, 8], mybir.dt.uint32)
    i8f = ctx.sb("i8f", [128, 2])
    gcol = ctx.sb("gcol", [128, 4])
    probs = ctx.sb("probs", [128, 8])
    srow = ctx.sb("srow", [128, 1])
    ones_col = ctx.sb("ones_col", [128, 1])
    ones_row = ctx.sb("ones_row", [1, 128])
    xc = ctx.sb("xc", [128, 512])
    sq = ctx.sb("sq", [128, 512])
    t1s = ctx.sb("t1s", [128, 512])
    t2s = ctx.sb("t2s", [128, 512])
    t2s2 = sq
    gel = ctx.sb("gel", [128, 512])
    gel2B = xc
    m_row = ctx.sb("m_row", [1, TPC])
    r_row = ctx.sb("r_row", [1, TPC])
    mr_row = ctx.sb("mr_row", [1, TPC])
    ident = ctx.sb("ident", [128, 128])
    ident_bf = ctx.sb("ident_bf", [128, 128], BF16)
    sb_small = {}
    for nm, ap_ in [("bo0", bo0), ("down_b", down_b), ("sb1", sb1), ("sb2", sb2),
                    ("upb", upb_in), ("coreb", coreb_in), ("bq1", bq1), ("bk1", bk1),
                    ("bv1", bv1), ("b1cat", b1cat), ("b2cat", b2cat)]:
        t = ctx.sb("s_" + nm, [int(s) for s in ap_.shape])
        sb_small[nm] = t
        emit_loads(ctx, "qx", [(t[:], ap_, "s_" + nm)], tag="small")
    oh16 = ctx.sb("oh16", [16, 2048])
    emit_loads(ctx, "qx", [(oh16[:], oh16_in, "oh16")], tag="oh")
    rb8 = ctx.sb("rb8", [8, 1])
    nlc8 = ctx.sb("nlc8", [8, 1])
    emit_loads(ctx, "qx", [(rb8[:], rb_in, "rb8"), (nlc8[:], nlc_in, "nlc8"),
                           (ident[:], ident_in, "ident")], tag="small2")
    bsum = ctx.sb("bsum", [128, 16])
    p.add("dve", lambda e: e.tensor_add(bsum[:], sb_small["upb"][:], sb_small["coreb"][:]),
          reads=["s_upb", "s_coreb"], writes=["bsum"], tag="bsum")
    p.add("dve", lambda e: e.tensor_add(rb8[:], rb8[:], nlc8[:]),
          reads=["rb8", "nlc8"], writes=["rb8"], tag="rbadj")
    p.add("dve", lambda e: e.memset(ones_col[:], 1.0), writes=["ones_col"], tag="oc")
    ones_col_bf = ctx.sb("ones_col_bf", [128, 1], BF16)
    p.add("dve", lambda e: e.memset(ones_col_bf[:], 1.0), writes=["ones_col_bf"], tag="ocb")
    p.add("dve", lambda e: e.memset(ones_row[:], 1.0), writes=["ones_row"], tag="orr")
    p.add("dve", lambda e: e.tensor_copy(ident_bf[:], ident[:]), reads=["ident"],
          writes=["ident_bf"], tag="identc")

    pf = [ctx.psum(f"pf{i}", [128, 512]) for i in range(6)]
    pfk = [f"pf{i}" for i in range(6)]
    pb = [ctx.psum(f"pbb{i}", [128, 1024], BF16) for i in range(2)]
    pbk = ["pbb0", "pbb1"]

    ws_g = WStream(ctx, "wsg", "qa", 2, [128, 2048])
    ws_u = WStream(ctx, "wsu", "qb", 2, [128, 2048])
    ws_d = WStream(ctx, "wsd", "qc", 2, [128, 1024])
    ws_k = WStream(ctx, "wsk", "qa", 2, [128, 4096])

    ln_consts = {"ones_col": (ones_col[:, 0:1], "ones_col"),
                 "ones_row": (ones_row[0:1, :], "ones_row"),
                 "xc": (xc, "xc"), "sq": (sq, "sq"), "t1": (t1s, "t1s"),
                 "m_row": (m_row, "m_row"), "r_row": (r_row, "r_row"),
                 "mr_row": (mr_row, "mr_row")}

    def dram_provider(dram, tagp, dkey=None):
        state = {"n": 0}

        def prov(i, passno):
            b = state["n"] % 2
            state["n"] += 1
            buf, key = RS[b], RSk[b]
            rd = [dkey] if dkey else []
            p.add(["qx", "qr"][state["n"] % 2], (lambda buf=buf, i=i: (lambda e: e.dma_start(
                out=buf[:], in_=dram[128 * i:128 * (i + 1), :])))(),
                reads=rd, writes=[key], tag=f"{tagp}_ld{state['n']}")
            return buf[:], [key]
        return prov

    def xn_sink(i, th, t1, t1k, mb2, mb2k):
        p.add("dve", (lambda i=i, th=th, t1=t1, mb2=mb2: (lambda e: e.tensor_tensor(
            BFA[i][:, 512 * th:512 * (th + 1)], t1[:, 0:512], mb2[:, 0:512], ALU.subtract)))(),
            reads=[t1k, mb2k], writes=[BFAk[i]], tag=f"xnsink_{i}_{th}")

    for i in range(NT):
        p.add("qx", (lambda i=i: (lambda e: e.dma_start(
            out=BFA[i][:], in_=q0t_in[128 * i:128 * (i + 1), :])))(),
            writes=[BFAk[i]], tag=f"q0ld{i}")
    p.add("dve", lambda e: e.memset(bdiag[:], 0.0), writes=["bdiag"], tag="bdz")
    for h in range(NH):
        kb, kbk = ws_k.load(k0nat[h].rearrange("(tt p) d -> p tt d", p=128), tag=f"k{h}")
        vb, vbk = ws_k.load(v0nat[h].rearrange("(tt p) d -> p tt d", p=128), tag=f"v{h}")
        pairs = [(kb[:, 128 * tt:128 * (tt + 1)], vb[:, 128 * tt:128 * (tt + 1)], [kbk, vbk])
                 for tt in range(32)]
        emit_mm_group(ctx, pf[0][:, 0:128], pfk[0], pairs, f"kv{h}")
        p.add("dve", (lambda h=h: (lambda e: e.tensor_copy(kvs[:, 128 * h:128 * (h + 1)],
                                                           pf[0][:, 0:128])))(),
              reads=[pfk[0]], writes=["kvs"], tag=f"kvev{h}")
        pairs = [(kb[:, 128 * tt:128 * (tt + 1)], ones_col_bf[:, 0:1], [kbk, "ones_col_bf"])
                 for tt in range(32)]
        emit_mm_group(ctx, pf[1][:, 0:1], pfk[1], pairs, f"ks{h}")
        p.add("dve", (lambda h=h: (lambda e: e.tensor_copy(
            bdiag[:, 16 * h + h:16 * h + h + 1], pf[1][:, 0:1])))(),
            reads=[pfk[1]], writes=["bdiag"], tag=f"ksev{h}")

    if B_STOP < 2:
        ctx.finish()
        return ctx
    for th in range(2):
        pairs = [(bdiag[:, 16 * kt:16 * (kt + 1)], BFA[kt][:, 512 * th:512 * (th + 1)],
                  ["bdiag", BFAk[kt]]) for kt in range(NT)]
        emit_mm_group(ctx, pf[2][0:16, 0:512], pfk[2], pairs, f"den{th}")
        p.add("dve", (lambda th=th: (lambda e: e.tensor_scalar(
            zrow[:, 512 * th:512 * (th + 1)], pf[2][0:16, 0:512], EPS_ATT0, None, ALU.add)))(),
            reads=[pfk[2]], writes=["zrow"], tag=f"zev{th}")
    p.add("dve", lambda e: e.reciprocal(zrow[:], zrow[:]), reads=["zrow"], writes=["zrow"],
          tag="zinv")
    for h in range(NH):
        for th in range(2):
            p.add("pe", (lambda h=h, th=th: (lambda e: e.matmul(
                pf[3][:, 0:512], kvs[:, 128 * h:128 * (h + 1)],
                BFA[h][:, 512 * th:512 * (th + 1)], start=True, stop=True)))(),
                reads=["kvs", BFAk[h]], writes=[pfk[3]], tag=f"num{h}_{th}")
            p.add("pe", (lambda h=h, th=th: (lambda e: e.matmul(
                pf[4][:, 0:512], oh16[0:16, 128 * h:128 * (h + 1)],
                zrow[0:16, 512 * th:512 * (th + 1)], start=True, stop=True)))(),
                reads=["oh16", "zrow"], writes=[pfk[4]], tag=f"zb{h}_{th}")
            p.add("dve", (lambda th=th: (lambda e: e.tensor_copy(
                gel[:, 0:512], pf[4][:, 0:512])))(),
                reads=[pfk[4]], writes=["gel"], tag=f"zbe{h}_{th}")
            p.add("dve", (lambda h=h, th=th: (lambda e: e.tensor_tensor(
                BFC[h][:, 512 * th:512 * (th + 1)], pf[3][:, 0:512], gel[:, 0:512],
                ALU.mult)))(),
                reads=[pfk[3], "gel"], writes=[BFCk[h]], tag=f"at0_{h}_{th}")
    x1prov = dram_provider(x1T, "x1s")
    for m in range(NT):
        wb, wk = ws_g.load(wo0_t[m].rearrange("(kt p) n -> p kt n", p=128), tag=f"wo{m}")
        xap, xkeys = x1prov(m, 9)
        for th in range(2):
            ps, psk = pf[th], pfk[th]
            pairs = [(wb[:, 128 * k:128 * (k + 1)], BFC[k][:, 512 * th:512 * (th + 1)],
                      [wk, BFCk[k]]) for k in range(NT)]
            emit_mm_group(ctx, ps[:, 0:512], psk, pairs, f"womm{m}_{th}")
            p.add("dve", (lambda m=m, th=th, ps=ps: (lambda e: e.tensor_scalar(
                t2s[:, 0:512], ps[:, 0:512], sb_small["bo0"][:, m:m + 1], None, ALU.add)))(),
                reads=[psk, "s_bo0"], writes=["t2s"], tag=f"woep{m}_{th}")
            p.add("dve", (lambda m=m, th=th, xap=xap: (lambda e: e.tensor_tensor(
                xap[:, 512 * th:512 * (th + 1)], xap[:, 512 * th:512 * (th + 1)],
                t2s[:, 0:512], ALU.add)))(),
                reads=["t2s"] + list(xkeys), writes=list(xkeys), tag=f"wores{m}_{th}")
        p.add("qo", (lambda m=m, xap=xap: (lambda e: e.dma_start(
            out=xr2[128 * m:128 * (m + 1), :], in_=xap)))(),
            reads=list(xkeys), writes=["xr2"], tag=f"x2st{m}")

    if B_STOP < 3:
        ctx.finish()
        return ctx
    emit_ln(ctx, "lnB3", dram_provider(xr2, "x2s", "xr2"), NT, TPC, ln_consts, pf, pfk, xn_sink)
    bslot = lambda i, th: (AC[i // 2][:, 512 * (i % 2):512 * (i % 2) + 512], ACk[i // 2])
    x2prov = dram_provider(xr2, "x2r", "xr2")
    for th in range(2):
        emit_ffn0(ctx, BFA, BFAk, BFB, BFBk, bslot, [(gel, "gel"), (gel2B, "xc")],
                  pf, pfk, ws_g, ws_u, ws_d, wg_t, wu_t, wd_t, f"Bf0h{th}", th_list=(th,))
        for i in range(NT):
            aap, akey = bslot(i, th)
            xap, xkeys = x2prov(i, 9)
            p.add("dve", (lambda i=i, th=th, aap=aap, xap=xap: (lambda e: e.scalar_tensor_tensor(
                aap, aap, 0.5, xap[:, 512 * th:512 * (th + 1)], ALU.mult, ALU.add)))(),
                reads=[akey] + list(xkeys), writes=[akey], tag=f"resid3_{i}_{th}")
            p.add("qo", (lambda i=i, th=th, aap=aap: (lambda e: e.dma_start(
                out=xr3[128 * i:128 * (i + 1), 512 * th:512 * (th + 1)], in_=aap)))(),
                reads=[akey], writes=["xr3"], tag=f"x3st{i}_{th}")

    if B_STOP < 4:
        ctx.finish()
        return ctx
    emit_ln(ctx, "lnB4", dram_provider(xr3, "x3s", "xr3"), NT, TPC, ln_consts, pf, pfk, xn_sink)
    T2S2K = "sq"
    E = {"BFA": BFA, "BFAk": BFAk, "BFB": BFB, "BFBk": BFBk, "BFC": BFC, "BFCk": BFCk,
         "AC": AC, "ACk": ACk, "pf": pf, "pfk": pfk, "ws_g": ws_g, "ws_d": ws_d, "ws_k": ws_k,
         "down_t": down_t, "router_t": router_t, "sw1_t": sw1_t, "sw2_t": sw2_t,
         "core_t": core_t, "up_t": up_t, "ew1_t": ew1_t, "ew2_t": ew2_t,
         "sb_small": sb_small, "rb8": rb8, "oh16": oh16, "ident": ident,
         "gel": gel, "t2s": t2s, "t2s2": (t2s2, T2S2K), "csb": csb, "Ccol": Ccol, "negc8": negc8, "b1g": b1g,
         "W8": W8, "m8t": m8t, "i8t": i8t, "i8f": i8f, "gcol": gcol, "probs": probs,
         "srow": srow, "wrows_bf": wrows_bf, "wrows_f": wrows_f, "lgT": lgT,
         "ones_row": ones_row, "bsum": bsum, "dram_provider": dram_provider}
    emit_moe(ctx, E, xr3, "xr3", [xr4, x4T_o], ["xr4", None], "B4")
    if B_STOP < 5:
        ctx.finish()
        return ctx
    emit_ln(ctx, "lnB5", dram_provider(xr4, "x4s", "xr4"), NT, TPC, ln_consts, pf, pfk, xn_sink)
    stage = [BFC[0], BFC[1], BFC[2], BFC[3]]
    stagek = [BFCk[0], BFCk[1], BFCk[2], BFCk[3]]
    kinds = [("q", wq1_t, sb_small["bq1"], "s_bq1", q1t_o),
             ("k", wk1_t, sb_small["bk1"], "s_bk1", k1t_o),
             ("v", wv1_t, sb_small["bv1"], "s_bv1", v1t_o)]
    kinds = kinds[:max(0, B_STOP - 4)]
    for kind, wt, bt, btk, outd in kinds:
        for m in range(NT):
            wb, wk = ws_g.load(wt[m].rearrange("(kt p) n -> p kt n", p=128), tag=f"q1{kind}{m}")
            st, stk = stage[m % 2], stagek[m % 2]
            for th in range(2):
                ps, psk = pf[th], pfk[th]
                pairs = [(wb[:, 128 * k2:128 * (k2 + 1)], BFA[k2][:, 512 * th:512 * (th + 1)],
                          [wk, BFAk[k2]]) for k2 in range(NT)]
                emit_mm_group(ctx, ps[:, 0:512], psk, pairs, f"p1{kind}{m}_{th}")
                scl = L1SCALE if kind in ("q", "k") else 1.0
                p.add("dve", (lambda m=m, th=th, ps=ps, bt=bt, scl=scl, st=st:
                              (lambda e: e.tensor_scalar(
                                  st[:, 512 * th:512 * (th + 1)], ps[:, 0:512],
                                  bt[:, m:m + 1], scl, ALU.add, ALU.mult)))(),
                    reads=[psk, btk], writes=[stk], tag=f"ep1{kind}{m}_{th}")
            p.add("qo", (lambda m=m, st=st, outd=outd: (lambda e: e.dma_start(
                out=outd[128 * m:128 * (m + 1), :], in_=st[:])))(),
                reads=[stk], tag=f"st1{kind}{m}")

    ctx.finish()
    return ctx


def host_B_inputs(inputs, W, a_res):
    ident = np.eye(128, dtype=np.float32)
    t16 = lambda b: np.ascontiguousarray(np.asarray(b).reshape(-1, 128).T)
    wo0_t = tile_lhsT(W["l0_wot"])
    wg_t = tile_lhsT(W["wgt"]); wu_t = tile_lhsT(W["wut"])
    wd_t = np.ascontiguousarray(W["wdt"].reshape(128, 128, 16, 128).transpose(2, 0, 1, 3))
    down_t = tile_lhsT(W["l1_down_w"])
    up_t = tile_lhsT(W["l1_up_w"])
    router_t = tile_lhsT(W["l1_router_w"])
    sw1_t = tile_lhsT(W["l1_shared_w1"]); sw2_t = tile_lhsT(W["l1_shared_w2"])
    core_t = tile_lhsT(W["l1_core_w"])
    ew1 = [tile_lhsT(W[f"l1_e{e}_w1"]) for e in range(8)]
    ew2 = [tile_lhsT(W[f"l1_e{e}_w2"]) for e in range(8)]
    wq1_t = tile_lhsT(W["l1_wqt"]); wk1_t = tile_lhsT(W["l1_wkt"]); wv1_t = tile_lhsT(W["l1_wvt"])
    b1cat = np.concatenate([t16(inputs[f"L1_ffn_e{e}_b1"]) for e in range(8)], axis=1)
    b2cat = np.concatenate([t16(inputs[f"L1_ffn_e{e}_b2"]) for e in range(8)], axis=1)
    nlc = -COST_LAMBDA * np.array([1024 * h + h * 1024 for h in HID], np.float32)
    maps = []
    for c in range(NCORE):
        bi = c // 4
        grp = range(4 * bi, 4 * bi + 4)
        k0 = np.concatenate([np.asarray(a_res[cc]["knat"]) for cc in grp], axis=1)
        v0 = np.concatenate([np.asarray(a_res[cc]["vnat"]) for cc in grp], axis=1)
        maps.append({
            "x1T": np.ascontiguousarray(np.asarray(a_res[c]["x1T"])),
            "q0t": np.ascontiguousarray(np.asarray(a_res[c]["qt"])),
            "k0nat": np.ascontiguousarray(k0), "v0nat": np.ascontiguousarray(v0),
            "wo0_t": wo0_t, "wg_t": wg_t, "wu_t": wu_t, "wd_t": wd_t,
            "down_t": down_t, "up_t": up_t, "router_t": router_t,
            "sw1_t": sw1_t, "sw2_t": sw2_t, "core_t": core_t,
            **{f"e{e}w1_t": ew1[e] for e in range(8)},
            **{f"e{e}w2_t": ew2[e] for e in range(8)},
            "wq1_t": wq1_t, "wk1_t": wk1_t, "wv1_t": wv1_t,
            "bo0": t16(inputs["L0_attn_bo"]),
            "down_b": t16(inputs["L1_ffn_down_b"]),
            "b1cat": b1cat, "b2cat": b2cat,
            "sb1": t16(inputs["L1_ffn_shared_b1"]), "sb2": t16(inputs["L1_ffn_shared_b2"]),
            "bsum_in": t16(inputs["L1_ffn_up_b"]),
            "upb_in": t16(inputs["L1_ffn_up_b"]), "coreb_in": t16(inputs["L1_ffn_core_b"]),
            "rb_in": np.ascontiguousarray(inputs["L1_ffn_router_b"].reshape(8, 1)),
            "nlc_in": np.ascontiguousarray(nlc.reshape(8, 1)),
            "bq1": t16(inputs["L1_attn_bq"]), "bk1": t16(inputs["L1_attn_bk"]),
            "bv1": t16(inputs["L1_attn_bv"]),
            "ident_in": ident,
            "oh16_in": np.kron(np.eye(16, dtype=np.float32), np.ones((1, 128), np.float32)),
        })
    return maps





def build_C():
    ctx = Ctx("C")
    p = ctx.p
    for d, b in [("qa", "sync"), ("qc", "sync"), ("qb", "scalar"), ("qx", "scalar"),
                 ("qr", "sync"), ("qo", "gpsimd")]:
        p.dma_domain(d, b)

    x4T = ctx.inp("x4T", [D, TPC])
    q1t_in = ctx.inp("q1t", [D, TPC], BF16)
    k1T = ctx.inp("k1T", [D, S], BF16)
    v1nat = ctx.inp("v1nat", [NH, S, DH], BF16)
    wo1_t = ctx.inp("wo1_t", [16, D, 128], BF16)
    down_t = ctx.inp("down_t", [8, D, 128], BF16)
    up_t = ctx.inp("up_t", [16, L1L, 128], BF16)
    router_t = ctx.inp("router_t", [1, D, 8], BF16)
    sw1_t = ctx.inp("sw1_t", [8, L1L, 128], BF16)
    sw2_t = ctx.inp("sw2_t", [8, L1L, 128], BF16)
    core_t = ctx.inp("core_t", [16, D, 128], BF16)
    ew1_t = [ctx.inp(f"e{e}w1_t", [HE_TILES[e], L1L, 128], BF16) for e in range(8)]
    ew2_t = [ctx.inp(f"e{e}w2_t", [8, HID[e], 128], BF16) for e in range(8)]
    bo1 = ctx.inp("bo1", [128, 16])
    down_b = ctx.inp("down_b", [128, 8])
    b1cat = ctx.inp("b1cat", [128, 160])
    b2cat = ctx.inp("b2cat", [128, 64])
    sb1 = ctx.inp("sb1", [128, 8])
    sb2 = ctx.inp("sb2", [128, 8])
    upb_in = ctx.inp("upb_in", [128, 16])
    coreb_in = ctx.inp("coreb_in", [128, 16])
    rb_in = ctx.inp("rb_in", [8, 1])
    nlc_in = ctx.inp("nlc_in", [8, 1])
    ident_in = ctx.inp("ident_in", [128, 128])
    oh16_in = ctx.inp("oh16_in", [16, 1024])
    bmask_in = ctx.inp("bmask_in", [128, 4992], BF16)

    x6T_o = ctx.out("x6T", [D, TPC])

    at1d = ctx.nc.dram_tensor("at1d", [D, TPC], BF16, kind="Internal").ap()
    xr5 = ctx.nc.dram_tensor("xr5", [D, TPC], F32, kind="Internal").ap()

    AC = [ctx.sb(f"AC{i}", [128, TPC]) for i in range(8)]
    ACk = [f"AC{i}" for i in range(8)]
    BFA = [ctx.sb(f"BFA{i}", [128, TPC], BF16) for i in range(NT)]
    BFAk = [f"BFA{i}" for i in range(NT)]
    BFB = [ctx.sb(f"BFB{i}", [128, TPC], BF16) for i in range(8)]
    BFBk = [f"BFB{i}" for i in range(8)]
    BFC = [ctx.sb(f"BFC{i}", [128, TPC], BF16) for i in range(NT)]
    BFCk = [f"BFC{i}" for i in range(NT)]
    RS = [ctx.sb(f"RS{i}", [128, TPC]) for i in range(2)]
    RSk = [f"RS{i}" for i in range(2)]
    ones_col = ctx.sb("ones_col", [128, 1])
    ones_row = ctx.sb("ones_row", [1, 128])
    xc = ctx.sb("xc", [128, 512])
    sq = ctx.sb("sq", [128, 512])
    t1s = ctx.sb("t1s", [128, 512])
    t2s = ctx.sb("t2s", [128, 512])
    t2s2 = sq
    sc32l = [(ctx.sb("sc32_0", [128, 512]), "sc32_0"), (t1s, "t1s")]
    gel = ctx.sb("gel", [128, 512])
    m_row = ctx.sb("m_row", [1, TPC])
    r_row = ctx.sb("r_row", [1, TPC])
    mr_row = ctx.sb("mr_row", [1, TPC])
    ident = ctx.sb("ident", [128, 128])
    ident_bf = ctx.sb("ident_bf", [128, 128], BF16)
    oh16 = ctx.sb("oh16", [16, 1024])
    bmask = ctx.sb("bmask", [128, 4992], BF16)
    pc16l = [(ctx.sb(f"pc16_{i}", [128, 512], BF16), f"pc16_{i}") for i in range(2)]
    mrow = ctx.sb("mrow", [128, 1])
    mneg = ctx.sb("mneg", [128, 1])
    lrow = ctx.sb("lrow", [128, 1])
    lsum = ctx.sb("lsum", [128, 1])
    linv_all = ctx.sb("linv_all", [128, 8])
    linv_rows = ctx.sb("linv_rows", [4, 128])
    zch = ctx.sb("zch", [128, 128])
    atst = ctx.sb("atst", [128, TPC], BF16)
    csb = ctx.sb("csb", [128, 64])
    Ccol = ctx.sb("Ccol", [128, 8])
    negc8 = ctx.sb("negc8", [8, 1024], BF16)
    b1g = ctx.sb("b1g", [128, 160], BF16)
    W8 = ctx.sb("W8", [128, 8])
    m8t = ctx.sb("m8t", [128, 8])
    i8t = ctx.sb("i8t", [128, 8], mybir.dt.uint32)
    i8f = ctx.sb("i8f", [128, 2])
    gcol = ctx.sb("gcol", [128, 4])
    probs = ctx.sb("probs", [128, 8])
    srow = ctx.sb("srow", [128, 1])
    zrow = ctx.sb("zrow", [16, TPC])
    lgT = zrow
    wrows_bf = ctx.sb("wrows_bf", [8, TPC], BF16)
    wrows_f = ctx.sb("wrows_f", [8, TPC])
    sb_small = {}
    for nm, ap_ in [("bo1", bo1), ("down_b", down_b), ("sb1", sb1), ("sb2", sb2),
                    ("upb", upb_in), ("coreb", coreb_in), ("b1cat", b1cat), ("b2cat", b2cat)]:
        t = ctx.sb("s_" + nm, [int(s) for s in ap_.shape])
        sb_small[nm] = t
        emit_loads(ctx, "qx", [(t[:], ap_, "s_" + nm)], tag="small")
    rb8 = ctx.sb("rb8", [8, 1])
    nlc8 = ctx.sb("nlc8", [8, 1])
    emit_loads(ctx, "qx", [(rb8[:], rb_in, "rb8"), (nlc8[:], nlc_in, "nlc8"),
                           (ident[:], ident_in, "ident"), (oh16[:], oh16_in, "oh16"),
                           (bmask[:], bmask_in, "bmask")],
               tag="small2")
    bsum = ctx.sb("bsum", [128, 16])
    p.add("dve", lambda e: e.tensor_add(bsum[:], sb_small["upb"][:], sb_small["coreb"][:]),
          reads=["s_upb", "s_coreb"], writes=["bsum"], tag="bsum")
    p.add("dve", lambda e: e.tensor_add(rb8[:], rb8[:], nlc8[:]),
          reads=["rb8", "nlc8"], writes=["rb8"], tag="rbadj")
    p.add("dve", lambda e: e.memset(ones_col[:], 1.0), writes=["ones_col"], tag="oc")
    p.add("dve", lambda e: e.memset(ones_row[:], 1.0), writes=["ones_row"], tag="orr")
    p.add("dve", lambda e: e.tensor_copy(ident_bf[:], ident[:]), reads=["ident"],
          writes=["ident_bf"], tag="identc")

    pf = [ctx.psum(f"pf{i}", [128, 512]) for i in range(6)]
    pfk = [f"pf{i}" for i in range(6)]
    pb = [ctx.psum(f"pbb{i}", [128, 1024], BF16) for i in range(2)]
    pbk = ["pbb0", "pbb1"]

    ws_g = WStream(ctx, "wsg", "qa", 2, [128, 2048])
    ws_d = WStream(ctx, "wsd", "qc", 2, [128, 1024])
    ws_k = WStream(ctx, "wsk", "qa", 2, [128, 4096])
    ws_v = ws_k

    ln_consts = {"ones_col": (ones_col[:, 0:1], "ones_col"),
                 "ones_row": (ones_row[0:1, :], "ones_row"),
                 "xc": (xc, "xc"), "sq": (sq, "sq"), "t1": (t1s, "t1s"),
                 "m_row": (m_row, "m_row"), "r_row": (r_row, "r_row"),
                 "mr_row": (mr_row, "mr_row")}

    def dram_provider(dram, tagp, dkey=None):
        state = {"n": 0}

        def prov(i, passno):
            b = state["n"] % 2
            state["n"] += 1
            buf, key = RS[b], RSk[b]
            rd = [dkey] if dkey else []
            p.add(["qx", "qr"][state["n"] % 2], (lambda buf=buf, i=i: (lambda e: e.dma_start(
                out=buf[:], in_=dram[128 * i:128 * (i + 1), :])))(),
                reads=rd, writes=[key], tag=f"{tagp}_ld{state['n']}")
            return buf[:], [key]
        return prov

    def xn_sink(i, th, t1, t1k, mb2, mb2k):
        p.add("dve", (lambda i=i, th=th, t1=t1, mb2=mb2: (lambda e: e.tensor_tensor(
            BFA[i][:, 512 * th:512 * (th + 1)], t1[:, 0:512], mb2[:, 0:512], ALU.subtract)))(),
            reads=[t1k, mb2k], writes=[BFAk[i]], tag=f"xnsink_{i}_{th}")

    for i in range(NT):
        p.add("qx", (lambda i=i: (lambda e: e.dma_start(
            out=BFA[i][:], in_=q1t_in[128 * i:128 * (i + 1), :])))(),
            writes=[BFAk[i]], tag=f"q1ld{i}")
    N_KC = 8
    N_KT = 32
    for h in range(NH):
        kb, kbk = ws_k.load(k1T[128 * h:128 * (h + 1), :], tag=f"k{h}")
        vb, vbk = ws_v.load(v1nat[h].rearrange("(tt p) d -> p tt d", p=128), tag=f"v{h}")
        for qg in range(2):
            for qt4 in range(4):
                qsl = slice(512 * qg + 128 * qt4, 512 * qg + 128 * (qt4 + 1))
                p.add("dve", lambda e: e.memset(mrow[:], -1e30), writes=["mrow"],
                      tag=f"mz{h}_{qg}_{qt4}")
                for kc in range(N_KC):
                    ps, psk = pf[kc % 2], pfk[kc % 2]
                    p.add("pe", (lambda h=h, qsl=qsl, kc=kc, ps=ps: (lambda e: e.matmul(
                        ps[:, 0:512], BFA[h][:, qsl], kb[:, 512 * kc:512 * (kc + 1)],
                        start=True, stop=True)))(),
                        reads=[BFAk[h], kbk], writes=[psk], tag=f"s1{h}_{qg}_{qt4}_{kc}")
                    p.add("dve", (lambda ps=ps: (lambda e: e.reduce_max(
                        lsum[:, 0:1], ps[:, 0:512], axis=X)))(),
                        reads=[psk], writes=["lsum"], tag=f"rm1{h}_{qg}_{qt4}_{kc}")
                    p.add("dve", lambda e: e.tensor_max(mrow[:, 0:1], mrow[:, 0:1], lsum[:, 0:1]),
                          reads=["mrow", "lsum"], writes=["mrow"], tag=f"mx1{h}_{qg}_{qt4}_{kc}")
                p.add("dve", lambda e: e.tensor_scalar(mneg[:, 0:1], mrow[:, 0:1], -1.0, None, ALU.mult),
                      reads=["mrow"], writes=["mneg"], tag=f"mn{h}_{qg}_{qt4}")
                p.add("dve", lambda e: e.memset(lrow[:], 0.0), writes=["lrow"],
                      tag=f"lz{h}_{qg}_{qt4}")
                for kc in range(N_KC):
                    o = 896 + 512 * kc - 512 * qg - 128 * qt4
                    ps, psk = pf[2 + kc % 2], pfk[2 + kc % 2]
                    sc, sck = sc32l[kc % 2]
                    pc, pck = pc16l[kc % 2]
                    p.add("pe", (lambda h=h, qsl=qsl, kc=kc, ps=ps: (lambda e: e.matmul(
                        ps[:, 0:512], BFA[h][:, qsl], kb[:, 512 * kc:512 * (kc + 1)],
                        start=True, stop=True)))(),
                        reads=[BFAk[h], kbk], writes=[psk], tag=f"s2{h}_{qg}_{qt4}_{kc}")
                    p.add("dve", (lambda ps=ps, o=o, sc=sc: (lambda e: e.tensor_tensor(
                        sc[:, 0:512], ps[:, 0:512], bmask[:, o:o + 512], ALU.add)))(),
                        reads=[psk, "bmask"], writes=[sck], tag=f"sm2{h}_{qg}_{qt4}_{kc}")
                    p.add("act", (lambda sc=sc, pc=pc: (lambda e: e.activation(
                        pc[:, 0:512], sc[:, 0:512], AF.Exp,
                        bias=mneg[:, 0:1], accum_out=lsum[:, 0:1])))(),
                        reads=[sck, "mneg"], writes=[pck, "lsum"],
                        tag=f"ex{h}_{qg}_{qt4}_{kc}")
                    p.add("dve", lambda e: e.tensor_add(lrow[:, 0:1], lrow[:, 0:1], lsum[:, 0:1]),
                          reads=["lrow", "lsum"], writes=["lrow"], tag=f"la{h}_{qg}_{qt4}_{kc}")
                    for j in range(4):
                        kt = 4 * kc + j
                        bank, bankk = pb[j % 2], pbk[j % 2]
                        p.add("pe", (lambda j=j, bank=bank, pc=pc, pck=pck: (lambda e: e.transpose(
                            bank[:, 0:128], pc[:, 128 * j:128 * (j + 1)], ident_bf[:])))(),
                            reads=[pck, "ident_bf"], writes=[bankk],
                            tag=f"pt{h}_{qg}_{qt4}_{kt}")
                        p.add("dve", (lambda kt=kt, qt4=qt4, bank=bank: (lambda e: e.tensor_copy(
                            BFC[kt // 2][:, 512 * (kt % 2) + 128 * qt4:512 * (kt % 2) + 128 * (qt4 + 1)],
                            bank[:, 0:128])))(),
                            reads=[bankk], writes=[BFCk[kt // 2]],
                            tag=f"pte{h}_{qg}_{qt4}_{kt}")
                p.add("dve", (lambda qg=qg, qt4=qt4: (lambda e: e.reciprocal(
                    linv_all[:, 4 * qg + qt4:4 * qg + qt4 + 1], lrow[:, 0:1])))(),
                    reads=["lrow"], writes=["linv_all"], tag=f"li{h}_{qg}_{qt4}")
            ps, psk = pf[4], pfk[4]
            pairs = [(vb[:, 128 * kt:128 * (kt + 1)],
                      BFC[kt // 2][:, 512 * (kt % 2):512 * (kt % 2) + 512], [vbk, BFCk[kt // 2]])
                     for kt in range(N_KT)]
            emit_mm_group(ctx, ps[:, 0:512], psk, pairs, f"av{h}_{qg}")
            p.add("pe", (lambda qg=qg: (lambda e: e.transpose(
                pf[5][0:4, 0:128], linv_all[:, 4 * qg:4 * (qg + 1)], ident[:])))(),
                reads=["linv_all", "ident"], writes=[pfk[5]], tag=f"litr{h}_{qg}")
            p.add("dve", lambda e: e.tensor_copy(linv_rows[0:4, :], pf[5][0:4, 0:128]),
                  reads=[pfk[5]], writes=["linv_rows"], tag=f"lie{h}_{qg}")
            for qt4 in range(4):
                p.add("pe", (lambda qt4=qt4: (lambda e: e.matmul(
                    pf[5][:, 0:128], oh16[0:4, 128 * qt4:128 * (qt4 + 1)],
                    linv_rows[0:4, :], start=True, stop=True)))(),
                    reads=["oh16", "linv_rows"], writes=[pfk[5]], tag=f"zb{h}_{qg}_{qt4}")
                p.add("dve", lambda e: e.tensor_copy(zch[:, 0:128], pf[5][:, 0:128]),
                      reads=[pfk[5]], writes=["zch"], tag=f"zbe{h}_{qg}_{qt4}")
                p.add("dve", (lambda qg=qg, qt4=qt4, ps=ps: (lambda e: e.tensor_tensor(
                    atst[:, 512 * qg + 128 * qt4:512 * qg + 128 * (qt4 + 1)],
                    ps[:, 128 * qt4:128 * (qt4 + 1)], zch[:, 0:128], ALU.mult)))(),
                    reads=[psk, "zch"], writes=["atst"], tag=f"at1{h}_{qg}_{qt4}")
        p.add("qo", (lambda h=h: (lambda e: e.dma_start(
            out=at1d[128 * h:128 * (h + 1), :], in_=atst[:])))(),
            reads=["atst"], writes=["at1d"], tag=f"at1st{h}")

    x4prov = dram_provider(x4T, "x4s")
    atld = {"n": 0}
    for m in range(NT):
        wb, wk = ws_g.load(wo1_t[m].rearrange("(kt p) n -> p kt n", p=128), tag=f"wo1{m}")
        xap, xkeys = x4prov(m, 9)
        if m == 0:
            for k in range(NT):
                p.add("qb", (lambda k=k: (lambda e: e.dma_start(
                    out=BFC[k][:], in_=at1d[128 * k:128 * (k + 1), :])))(),
                    reads=["at1d"], writes=[BFCk[k]], tag=f"at1ld{k}")
        for th in range(2):
            ps, psk = pf[th], pfk[th]
            pairs = [(wb[:, 128 * k:128 * (k + 1)], BFC[k][:, 512 * th:512 * (th + 1)],
                      [wk, BFCk[k]]) for k in range(NT)]
            emit_mm_group(ctx, ps[:, 0:512], psk, pairs, f"wo1mm{m}_{th}")
            p.add("dve", (lambda m=m, th=th, ps=ps: (lambda e: e.tensor_scalar(
                t2s[:, 0:512], ps[:, 0:512], sb_small["bo1"][:, m:m + 1], None, ALU.add)))(),
                reads=[psk, "s_bo1"], writes=["t2s"], tag=f"wo1ep{m}_{th}")
            p.add("dve", (lambda m=m, th=th, xap=xap: (lambda e: e.tensor_tensor(
                xap[:, 512 * th:512 * (th + 1)], xap[:, 512 * th:512 * (th + 1)],
                t2s[:, 0:512], ALU.add)))(),
                reads=["t2s"] + list(xkeys), writes=list(xkeys), tag=f"wo1res{m}_{th}")
        p.add("qo", (lambda m=m, xap=xap: (lambda e: e.dma_start(
            out=xr5[128 * m:128 * (m + 1), :], in_=xap)))(),
            reads=list(xkeys), writes=["xr5"], tag=f"x5st{m}")

    emit_ln(ctx, "lnC3", dram_provider(xr5, "x5s", "xr5"), NT, TPC, ln_consts, pf, pfk, xn_sink)
    T2S2K = "sq"
    E = {"BFA": BFA, "BFAk": BFAk, "BFB": BFB, "BFBk": BFBk, "BFC": BFC, "BFCk": BFCk,
         "AC": AC, "ACk": ACk, "pf": pf, "pfk": pfk, "ws_g": ws_g, "ws_d": ws_d, "ws_k": ws_k,
         "down_t": down_t, "router_t": router_t, "sw1_t": sw1_t, "sw2_t": sw2_t,
         "core_t": core_t, "up_t": up_t, "ew1_t": ew1_t, "ew2_t": ew2_t,
         "sb_small": sb_small, "rb8": rb8, "oh16": oh16, "ident": ident,
         "gel": gel, "t2s": t2s, "t2s2": (t2s2, T2S2K), "csb": csb, "Ccol": Ccol, "negc8": negc8, "b1g": b1g,
         "W8": W8, "m8t": m8t, "i8t": i8t, "i8f": i8f, "gcol": gcol, "probs": probs,
         "srow": srow, "wrows_bf": wrows_bf, "wrows_f": wrows_f, "lgT": lgT,
         "ones_row": ones_row, "bsum": bsum, "dram_provider": dram_provider}
    emit_moe(ctx, E, xr5, "xr5", [x6T_o], [None], "C3")

    ctx.finish()
    return ctx


def host_C_inputs(inputs, W, b_res):
    ident = np.eye(128, dtype=np.float32)
    t16 = lambda b: np.ascontiguousarray(np.asarray(b).reshape(-1, 128).T)
    wo1_t = tile_lhsT(W["l1_wot"])
    down_t = tile_lhsT(W["l1_down_w"])
    up_t = tile_lhsT(W["l1_up_w"])
    router_t = tile_lhsT(W["l1_router_w"])
    sw1_t = tile_lhsT(W["l1_shared_w1"]); sw2_t = tile_lhsT(W["l1_shared_w2"])
    core_t = tile_lhsT(W["l1_core_w"])
    ew1 = [tile_lhsT(W[f"l1_e{e}_w1"]) for e in range(8)]
    ew2 = [tile_lhsT(W[f"l1_e{e}_w2"]) for e in range(8)]
    b1cat = np.concatenate([t16(inputs[f"L1_ffn_e{e}_b1"]) for e in range(8)], axis=1)
    b2cat = np.concatenate([t16(inputs[f"L1_ffn_e{e}_b2"]) for e in range(8)], axis=1)
    nlc = -COST_LAMBDA * np.array([1024 * h + h * 1024 for h in HID], np.float32)
    oh16 = np.kron(np.eye(16, dtype=np.float32), np.ones((1, 128), np.float32))[:, 0:1024]
    oh16 = np.ascontiguousarray(oh16)
    maps = []
    for c in range(NCORE):
        t0 = (c % 4) * TPC
        i_idx = np.arange(128)[:, None]
        v_idx = np.arange(4992)[None, :]
        bmask = np.where(v_idx <= i_idx + t0 + 896, np.float32(0.0),
                         np.float32(-1e30)).astype(ml_dtypes.bfloat16)
        bmask = np.ascontiguousarray(bmask)
        bi = c // 4
        grp = range(4 * bi, 4 * bi + 4)
        k1 = np.concatenate([np.asarray(b_res[cc]["k1t"]) for cc in grp], axis=1)
        v1h = []
        for h in range(NH):
            v1h.append(np.concatenate(
                [np.asarray(b_res[cc]["v1t"])[128 * h:128 * (h + 1), :].T for cc in grp], axis=0))
        v1nat = np.stack(v1h)
        maps.append({
            "x4T": np.ascontiguousarray(np.asarray(b_res[c]["x4T"])),
            "q1t": np.ascontiguousarray(np.asarray(b_res[c]["q1t"])),
            "k1T": np.ascontiguousarray(k1),
            "v1nat": np.ascontiguousarray(v1nat),
            "wo1_t": wo1_t, "down_t": down_t, "up_t": up_t, "router_t": router_t,
            "sw1_t": sw1_t, "sw2_t": sw2_t, "core_t": core_t,
            **{f"e{e}w1_t": ew1[e] for e in range(8)},
            **{f"e{e}w2_t": ew2[e] for e in range(8)},
            "bo1": t16(inputs["L1_attn_bo"]),
            "down_b": t16(inputs["L1_ffn_down_b"]),
            "b1cat": b1cat, "b2cat": b2cat,
            "sb1": t16(inputs["L1_ffn_shared_b1"]), "sb2": t16(inputs["L1_ffn_shared_b2"]),
            "upb_in": t16(inputs["L1_ffn_up_b"]), "coreb_in": t16(inputs["L1_ffn_core_b"]),
            "rb_in": np.ascontiguousarray(inputs["L1_ffn_router_b"].reshape(8, 1)),
            "nlc_in": np.ascontiguousarray(nlc.reshape(8, 1)),
            "ident_in": ident, "oh16_in": oh16, "bmask_in": bmask,
        })
    return maps



def host_A_inputs(inputs, W):
    wg_t = tile_lhsT(W["wgt"])
    wu_t = tile_lhsT(W["wut"])
    wdt = W["wdt"]
    wd_t = np.ascontiguousarray(wdt.reshape(128, 128, 16, 128).transpose(2, 0, 1, 3))
    wq_t = tile_lhsT(W["l0_wqt"])
    wk_t = tile_lhsT(W["l0_wkt"])
    wv_t = tile_lhsT(W["l0_wvt"])
    ident = np.eye(128, dtype=np.float32)
    x = inputs["L0_attn_x"]
    decay_full = np.exp(-(np.arange(S, dtype=np.float32) / np.float32(S))).astype(np.float32)
    maps = []
    for c in range(NCORE):
        b, t0 = c // 4, (c % 4) * TPC
        maps.append({
            "xT": np.ascontiguousarray(x[b, t0:t0 + TPC, :].T),
            "wg_t": wg_t, "wu_t": wu_t, "wd_t": wd_t,
            "wq_t": wq_t, "wk_t": wk_t, "wv_t": wv_t,
            "bq": np.ascontiguousarray(inputs["L0_attn_bq"].reshape(16, 128).T),
            "bk": np.ascontiguousarray(inputs["L0_attn_bk"].reshape(16, 128).T),
            "bv": np.ascontiguousarray(inputs["L0_attn_bv"].reshape(16, 128).T),
            "decay": np.ascontiguousarray(decay_full[t0:t0 + TPC].reshape(1, TPC)),
            "ident_in": ident,
        })
    return maps


def kernel(**inputs):
    inputs = {k: np.asarray(v) for k, v in inputs.items()}
    cores = list(range(NCORE))
    prep_ctx = build_prep()
    prep_res = run_launch(prep_ctx, host_prep_inputs(inputs), cores)
    W = host_prep_assemble(prep_res)
    ctx_a = build_A(0)
    a_res = run_launch(ctx_a, host_A_inputs(inputs, W), cores)
    a_res = [{k: np.asarray(v) for k, v in r.items()} for r in a_res]
    ctx_b = build_B()
    b_res = run_launch(ctx_b, host_B_inputs(inputs, W, a_res), cores)
    b_res = [{k: np.asarray(v) for k, v in r.items()} for r in b_res]
    ctx_c = build_C()
    c_res = run_launch(ctx_c, host_C_inputs(inputs, W, b_res), cores)
    out = np.empty((BATCH, S, D), np.float32)
    for c in cores:
        b, t0 = c // 4, (c % 4) * TPC
        out[b, t0:t0 + TPC, :] = np.asarray(c_res[c]["x6T"]).T
    return out
